# version 05
# baseline (speedup 1.0000x reference)
import contextlib
import math
import numpy as np
import ml_dtypes
import concourse.bass as bass
import concourse.mybir as mybir
from concourse.bass_utils import run_bass_kernel_spmd

F32 = mybir.dt.float32
BF16 = mybir.dt.bfloat16
AF = mybir.ActivationFunctionType
ALU = mybir.AluOpType
AX = mybir.AxisListType

B = 2
S = 8192
D = 1024
NCH = 4
NCORES = 8
WIN = 256

bf16 = ml_dtypes.bfloat16
LN_DEBUG_NOP = False


def cdiv(a, b):
    return (a + b - 1) // b


class PsRot:

    def __init__(self, bd, tiles, keys):
        self.bd = bd
        self.tiles = tiles
        self.keys = keys
        self.i = 0
        self.pending = {}

    def acquire(self):
        j = self.i % len(self.tiles)
        if j in self.pending:
            s, v = self.pending[j]
            self.bd.wait("tensor", s, v)
        else:
            self.bd.guard_wait("tensor", self.keys[j])
        self.i += 1
        return self.tiles[j]

    def release(self, tok):
        j = (self.i - 1) % len(self.tiles)
        self.pending[j] = tok
        self.bd.guard_set(self.keys[j], tok[0], tok[1])


class Bld:
    ENGINES = ("tensor", "vector", "scalar", "gpsimd", "sync")
    CHAINED = ("vector", "scalar", "gpsimd")

    def __init__(self, nc, es):
        self.nc = nc
        self.es = es
        self.ops = {e: [] for e in self.ENGINES}
        self.semh = {}
        self.semc = {}
        self.selfc = {e: 0 for e in self.CHAINED}
        self.psguard = {}
        self.alias = {}
        for e in self.CHAINED:
            self.sem("self_" + e)

    def sem(self, name):
        if name not in self.semh:
            self.semh[name] = self.es.enter_context(self.nc.semaphore(name))
            self.semc[name] = 0
        return self.semh[name]

    def cnt(self, name):
        if name in self.alias:
            return self.semc.get(name, 0)
        self.sem(name)
        return self.semc[name]

    def on(self, eng, fn):
        self.ops[eng].append(fn)

    def tok(self, eng):
        return ("self_" + eng, self.selfc[eng])

    def wait(self, eng, sem, val=None):
        if sem in self.alias:
            aeng, snaps = self.alias[sem]
            idx = (len(snaps) if val is None else val)
            if idx <= 0:
                return
            h = self.semh[aeng]
            v = snaps[idx - 1]
            self.on(eng, lambda e, h=h, v=v: e.wait_ge(h, v))
            return
        h = self.sem(sem)
        v = self.semc[sem] if val is None else val
        self.on(eng, lambda e, h=h, v=v: e.wait_ge(h, v))

    def wait_tok(self, eng, token):
        if token is None:
            return
        h = self.sem(token[0])
        v = token[1]
        self.on(eng, lambda e, h=h, v=v: e.wait_ge(h, v))

    def emit(self, eng, fn, inc=None, inc_n=1):
        if eng in self.CHAINED:
            sh = self.semh["self_" + eng]
            k = self.selfc[eng]
            self.selfc[eng] = k + 1
            self.semc["self_" + eng] = k + 1
            if inc is not None:
                a = self.alias.setdefault(inc, ("self_" + eng, []))
                assert a[0] == "self_" + eng, f"alias {inc} engine mix"
                a[1].append(k + 1)
                self.semc[inc] = self.semc.get(inc, 0) + inc_n
            self.on(eng, lambda e, sh=sh, k=k, fn=fn:
                    fn(e)._wait_ge(sh, k).then_inc(sh, 1))
            return ("self_" + eng, k + 1)
        if inc is None:
            self.on(eng, fn)
            return None
        h = self.sem(inc)
        self.semc[inc] += inc_n
        self.on(eng, lambda e, h=h, fn=fn, n=inc_n: fn(e).then_inc(h, n))
        return self.semc[inc]

    def dma(self, eng, out, in_, sem, **kw):
        h = self.sem(sem)
        self.semc[sem] += 16
        self.on(eng, lambda e, h=h, out=out, in_=in_, kw=kw:
                e.dma_start(out=out, in_=in_, **kw).then_inc(h, 16))
        return self.semc[sem]

    def guard_wait(self, eng, key):
        g = self.psguard.get(key)
        if g is not None:
            if g[0] in self.alias:
                aeng, snaps = self.alias[g[0]]
                if g[1] <= 0 or not snaps:
                    return
                h = self.semh[aeng]
                v = snaps[min(g[1], len(snaps)) - 1]
                self.on(eng, lambda e, h=h, v=v: e.wait_ge(h, v))
            else:
                self.wait(eng, g[0], g[1])

    def guard_set(self, key, sem, val=None):
        self.psguard[key] = (sem, self.semc.get(sem, 0) if val is None else val)

    def sb(self, name, shape, dtype):
        return self.es.enter_context(self.nc.sbuf_tensor(name, list(shape), dtype))

    def psum(self, name, shape, dtype=F32):
        return self.es.enter_context(self.nc.psum_tensor(name, list(shape), dtype))

    def finalize(self):
        block = self.es.enter_context(self.nc.Block())
        for eng in self.ENGINES:
            ops = self.ops[eng]
            if ops:
                getattr(block, eng)(lambda e, ops=ops: [f(e) for f in ops])


def make_ln_ctx(bd, T, tag="", ps_R=None, ps_M=None, ps_small=None):
    ctx = {}
    ctx["T"] = T
    ctx["xb"] = [[bd.sb(f"ln_xb{j}_{i}{tag}", [128, 512], BF16) for i in range(8)]
                 for j in range(2)]
    ctx["sqb"] = [[bd.sb(f"ln_sqb{j}_{i}{tag}", [128, 512], BF16) for i in range(8)]
                  for j in range(2)]
    ctx["ones_col"] = bd.sb(f"ln_onec{tag}", [128, 1], BF16)
    ctx["ones_row"] = bd.sb(f"ln_oner{tag}", [1, 128], F32)
    ctx["mrow"] = bd.sb(f"ln_mrow{tag}", [1, T], F32)
    ctx["rrow"] = bd.sb(f"ln_rrow{tag}", [1, T], F32)
    ctx["trow"] = bd.sb(f"ln_trow{tag}", [1, T], F32)
    ctx["tmp"] = bd.sb(f"ln_tmp{tag}", [128, 512], F32)
    if ps_small is None:
        ps_small = bd.psum(f"ln_pssm{tag}", [128, 512])
    ctx["ps_m"] = ps_small[0:1, 0:256]
    ctx["ps_s"] = ps_small[0:1, 256:512]
    ctx["ps_R"] = ps_R if ps_R is not None else [bd.psum(f"ln_psR{j}{tag}", [128, 512]) for j in range(2)]
    ctx["ps_M"] = ps_M if ps_M is not None else [bd.psum(f"ln_psM{j}{tag}", [128, 512]) for j in range(2)]
    ctx["_init"] = False
    return ctx


def emit_ln_init(bd, ctx):
    if ctx["_init"]:
        return
    ctx["_init"] = True
    bd.emit("gpsimd", lambda e: e.memset(ctx["ones_col"][:, :], 1.0))
    bd.emit("gpsimd", lambda e: e.memset(ctx["ones_row"][:, :], 1.0), inc="ln_ones")


def emit_ln(bd, ctx, x_sb, out_sb, nb, g_col=None, b_col=None, eps=1e-5,
            ready=None, T=None, x_off=0, out_off=0):
    if not isinstance(x_off, (list, tuple)):
        x_off = [x_off] * 8
    """x_sb/out_sb: lists of 8 tiles [128, >=T(+off)]. LN over the 8 partition
    tiles for columns [x_off, x_off+T) -> out columns [out_off, out_off+T).
    nb: unique sem prefix. ready: (sem, val) gating x availability."""
    nc = bd.nc
    T = T if T is not None else ctx["T"]
    xb, sqb = ctx["xb"], ctx["sqb"]
    mrow, rrow, trow = ctx["mrow"], ctx["rrow"], ctx["trow"]
    ps_m, ps_s, ps_R, ps_M = ctx["ps_m"], ctx["ps_s"], ctx["ps_R"], ctx["ps_M"]
    ones_col, ones_row = ctx["ones_col"], ctx["ones_row"]
    NQ = cdiv(T, 512)
    emit_ln_init(bd, ctx)
    if ready is not None:
        bd.wait("vector", ready[0], ready[1])
    bd.wait("tensor", "ln_ones")
    bd.guard_wait("tensor", "lnstats")
    for q in range(NQ):
        w = min(512, T - q * 512)
        jj = q % 2
        if q >= 2:
            bd.wait("vector", nb + "mm", 2 * (q - 1))
        for i in range(8):
            bd.emit("vector", lambda e, i=i, jj=jj, q=q, w=w: e.tensor_copy(
                xb[jj][i][:, 0:w], x_sb[i][:, x_off[i] + q * 512: x_off[i] + q * 512 + w]),
                inc=(nb + "cp") if i == 7 else None)
        bd.wait("scalar", nb + "cp")
        for i in range(8):
            bd.emit("scalar", lambda e, i=i, jj=jj, w=w: e.activation(
                sqb[jj][i][:, 0:w], xb[jj][i][:, 0:w], AF.Square),
                inc=(nb + "sq") if i == 7 else None)
        bd.wait("tensor", nb + "sq")
        for h in range(0, w, 256):
            w2 = min(256, w - h)
            for i in range(8):
                bd.emit("tensor", lambda e, i=i, jj=jj, h=h, w2=w2: e.matmul(
                    ps_m[0:1, 0:w2], ones_col[:, :], xb[jj][i][:, h:h + w2],
                    start=(i == 0), stop=(i == 7)))
            for i in range(8):
                bd.emit("tensor", lambda e, i=i, jj=jj, h=h, w2=w2: e.matmul(
                    ps_s[0:1, 0:w2], ones_col[:, :], sqb[jj][i][:, h:h + w2],
                    start=(i == 0), stop=(i == 7)),
                    inc=(nb + "mm") if i == 7 else None)
            bd.wait("vector", nb + "mm")
            bd.emit("vector", lambda e, q=q, h=h, w2=w2: e.tensor_scalar_mul(
                mrow[0:1, q * 512 + h:q * 512 + h + w2], ps_m[0:1, 0:w2], 1.0 / D))
            bd.emit("vector", lambda e, q=q, h=h, w2=w2: e.tensor_scalar_mul(
                rrow[0:1, q * 512 + h:q * 512 + h + w2], ps_s[0:1, 0:w2], 1.0 / D),
                inc=nb + "ev")
            bd.wait("tensor", nb + "ev")
    bd.emit("vector", lambda e: e.tensor_mul(trow[0:1, 0:T], mrow[0:1, 0:T], mrow[0:1, 0:T]))
    bd.emit("vector", lambda e: e.scalar_tensor_tensor(
        rrow[0:1, 0:T], rrow[0:1, 0:T], eps, trow[0:1, 0:T], ALU.add, ALU.subtract),
        inc=nb + "var")
    bd.wait("scalar", nb + "var")
    bd.emit("scalar", lambda e: e.activation(trow[0:1, 0:T], rrow[0:1, 0:T], AF.Ln))
    bd.emit("scalar", lambda e: e.activation(rrow[0:1, 0:T], trow[0:1, 0:T], AF.Exp, scale=-0.5),
            inc=nb + "rs")
    bd.wait("vector", nb + "rs")
    bd.emit("vector", lambda e: e.scalar_tensor_tensor(
        mrow[0:1, 0:T], mrow[0:1, 0:T], -1.0, rrow[0:1, 0:T], ALU.mult, ALU.mult),
        inc=nb + "st")
    bd.wait("tensor", nb + "st")
    for q in range(NQ):
        w = min(512, T - q * 512)
        sl = slice(q * 512, q * 512 + w)
        j = q % 2
        if q < 2:
            bd.guard_wait("tensor", f"lnR{j}")
            bd.guard_wait("tensor", f"lnM{j}")
        if q >= 2:
            bd.wait("tensor", nb + "use", (q - 1))
        bd.emit("tensor", lambda e, j=j, sl=sl, w=w: e.matmul(
            ps_R[j][:, 0:w], ones_row[:, :], rrow[0:1, sl], start=True, stop=True))
        bd.emit("tensor", lambda e, j=j, sl=sl, w=w: e.matmul(
            ps_M[j][:, 0:w], ones_row[:, :], mrow[0:1, sl], start=True, stop=True),
            inc=nb + "bc")
        bd.wait("vector", nb + "bc")
        for i in range(8):
            if g_col is not None:
                bd.emit("vector", lambda e, i=i, j=j, q=q, w=w: e.scalar_tensor_tensor(
                    ctx["tmp"][:, 0:w], x_sb[i][:, x_off[i] + q * 512: x_off[i] + q * 512 + w],
                    g_col[i][:, :], ps_R[j][:, 0:w], ALU.mult, ALU.mult))
            else:
                bd.emit("vector", lambda e, i=i, j=j, q=q, w=w: e.tensor_mul(
                    ctx["tmp"][:, 0:w], x_sb[i][:, x_off[i] + q * 512: x_off[i] + q * 512 + w],
                    ps_R[j][:, 0:w]))
            if b_col is not None:
                bd.emit("vector", lambda e, i=i, j=j, q=q, w=w: e.scalar_tensor_tensor(
                    out_sb[i][:, out_off + q * 512: out_off + q * 512 + w],
                    ctx["tmp"][:, 0:w], b_col[i][:, :], ps_M[j][:, 0:w], ALU.add, ALU.add),
                    inc=(nb + "use") if i == 7 else None)
            else:
                bd.emit("vector", lambda e, i=i, j=j, q=q, w=w: e.tensor_add(
                    out_sb[i][:, out_off + q * 512: out_off + q * 512 + w],
                    ctx["tmp"][:, 0:w], ps_M[j][:, 0:w]),
                    inc=(nb + "use") if i == 7 else None)
    for j in range(2):
        bd.guard_set(f"lnR{j}", nb + "use")
        bd.guard_set(f"lnM{j}", nb + "use")
    bd.guard_set("lnstats", nb + "ev")


def build_pln(T):
    nc = bass.Bass("TRN2", target_bir_lowering=False)
    X = nc.dram_tensor("x", [D, T], F32, kind="ExternalInput").ap()
    Y = nc.dram_tensor("y", [D, T], F32, kind="ExternalOutput").ap()
    es = contextlib.ExitStack()
    with es:
        bd = Bld(nc, es)
        x_sb = [bd.sb(f"x{i}", [128, T], F32) for i in range(8)]
        y_sb = [bd.sb(f"y{i}", [128, T], F32) for i in range(8)]
        ctx = make_ln_ctx(bd, T)
        for i in range(8):
            bd.dma("sync", x_sb[i][:, :], X[128 * i:128 * (i + 1), :], "ld")
        emit_ln(bd, ctx, x_sb, y_sb, "A", ready=("ld", bd.cnt("ld")))
        bd.wait("sync", "Ause")
        for i in range(8):
            bd.dma("sync", Y[128 * i:128 * (i + 1), :], y_sb[i][:, :], "st")
        bd.wait("sync", "st")
        bd.finalize()
    return nc


def build_pax(T, s):
    nc = bass.Bass("TRN2", target_bir_lowering=False)
    X = nc.dram_tensor("x", [D, T], F32, kind="ExternalInput").ap()
    Y = nc.dram_tensor("y", [D, T], F32, kind="ExternalInput").ap()
    O = nc.dram_tensor("o", [D, T], F32, kind="ExternalOutput").ap()
    es = contextlib.ExitStack()
    with es:
        bd = Bld(nc, es)
        x_sb = [bd.sb(f"x{i}", [128, T], F32) for i in range(8)]
        y_sb = [bd.sb(f"y{i}", [128, T], F32) for i in range(8)]
        o_sb = [bd.sb(f"o{i}", [128, T], F32) for i in range(8)]
        for i in range(8):
            bd.dma("sync", x_sb[i][:, :], X[128 * i:128 * (i + 1), :], "ld")
            bd.dma("sync", y_sb[i][:, :], Y[128 * i:128 * (i + 1), :], "ld")
        bd.wait("vector", "ld")
        for i in range(8):
            bd.emit("vector", lambda e, i=i: e.scalar_tensor_tensor(
                o_sb[i][:, :], y_sb[i][:, :], s, x_sb[i][:, :], ALU.mult, ALU.add),
                inc="cm")
        bd.wait("sync", "cm")
        for i in range(8):
            bd.dma("sync", O[128 * i:128 * (i + 1), :], o_sb[i][:, :], "st")
        bd.wait("sync", "st")
        bd.finalize()
    return nc


_prog_cache = {}


def get_prog(name, builder, *args):
    key = (name,) + args
    if key not in _prog_cache:
        _prog_cache[key] = builder(*args)
    return _prog_cache[key]


def run8(nc, in_maps):
    res = run_bass_kernel_spmd(nc, in_maps, core_ids=list(range(NCORES)))
    return res.results


def emit_mm(bd, nb, KT, MT, T, lhsT_fn, rhs_fn, evac_fn, ps_tiles, n_chunk=512,
            mwidth=128, extra_fn=None, ps_keys=None):
    NQ = cdiv(T, n_chunk)
    NB = len(ps_tiles)
    if ps_keys is None:
        ps_keys = [f"psk{id(p)}" for p in ps_tiles]
    tile_idx = 0
    for mt in range(MT):
        for q in range(NQ):
            w = min(n_chunk, T - q * n_chunk)
            ps = ps_tiles[tile_idx % NB]
            if tile_idx < NB:
                bd.guard_wait("tensor", ps_keys[tile_idx % NB])
            else:
                bd.wait("tensor", nb + "ev", tile_idx - NB + 1)
            for kt in range(KT):
                lt = lhsT_fn(kt, mt)
                rt = rhs_fn(kt, q, w)
                is_last = (kt == KT - 1) and (extra_fn is None)
                bd.emit("tensor", lambda e, lt=lt, rt=rt, ps=ps, w=w, kt=kt, il=is_last:
                        e.matmul(ps[:, 0:w], lt, rt, start=(kt == 0), stop=il),
                        inc=(nb + "mm") if is_last else None)
            if extra_fn is not None:
                extra_fn(mt, q, w, ps)
            evac_fn(mt, q, w, ps)
            tile_idx += 1
    for k in ps_keys:
        bd.guard_set(k, nb + "ev")


def emit_transpose_load(bd, nb, W_ap, R, C, wt_tiles, nat_buf, iden, ps_t,
                        dma_eng="gpsimd", evac_eng="vector"):
    RT, CT = cdiv(R, 128), cdiv(C, 128)
    t = 0
    bd.wait("tensor", "iden")
    for r in range(RT):
        nbuf = nat_buf[r % 2]
        rh = min(128, R - r * 128)
        if r >= 2:
            bd.wait(dma_eng, nb + "tp", (r - 1) * CT)
        else:
            bd.guard_wait(dma_eng, f"natbuf{r % 2}")
        bd.dma(dma_eng, nbuf[0:rh, 0:C], W_ap[r * 128:r * 128 + rh, 0:C],
               nb + f"ld{r % 2}")
        bd.wait("tensor", nb + f"ld{r % 2}")
        for c in range(CT):
            ch = min(128, C - c * 128)
            ps = ps_t[t % 2]
            if t < 2:
                bd.guard_wait("tensor", f"pstr{t % 2}")
            if t >= 2:
                bd.wait("tensor", nb + "te", t - 1)
            def _tr(e, ps=ps, nbuf=nbuf, rh=rh, c=c, ch=ch):
                with bd.nc.allow_low_precision(reason="bf16 PE transpose (pure data movement)"):
                    return e.transpose(ps[0:ch, 0:rh], nbuf[0:rh, c * 128:c * 128 + ch],
                                       iden[0:rh, 0:rh])
            bd.emit("tensor", _tr, inc=nb + "tp")
            bd.wait(evac_eng, nb + "tp")
            bd.emit(evac_eng, lambda e, ps=ps, r=r, c=c, rh=rh, ch=ch:
                    e.tensor_copy(wt_tiles[c][0:ch, r * 128:r * 128 + rh], ps[0:ch, 0:rh]),
                    inc=nb + "te")
            t += 1
    for j in range(2):
        bd.guard_set(f"pstr{j}", nb + "te")
        bd.guard_set(f"natbuf{j}", nb + "tp")


def emit_proj_stream(bd, nb, wt_set, rhs_tiles, out_hbm, T, otile, ps_mm,
                     out_tiles=None):
    NQ = cdiv(T, 512)

    def lhsT(kt, mt):
        return wt_set[kt][:, 128 * mt:128 * (mt + 1)]

    def rhs(kt, q, w):
        return rhs_tiles[kt][:, q * 512:q * 512 + w]

    def evac(mt, q, w, ps):
        ti = mt * NQ + q
        j = ti % 4
        bd.wait("vector", nb + "mm")
        if out_hbm is not None:
            if ti >= 4:
                bd.wait("vector", nb + f"st{j}", 16 * (ti // 4))
            else:
                bd.guard_wait("vector", f"otile{j}")
            bd.emit("vector", lambda e, j=j, ps=ps, w=w: e.tensor_copy(
                otile[j][:, 0:w], ps[:, 0:w]),
                inc=(None if out_tiles is not None else nb + "ev"))
            if out_tiles is not None:
                bd.emit("vector", lambda e, mt=mt, q=q, ps=ps, w=w: e.tensor_copy(
                    out_tiles[mt][:, q * 512:q * 512 + w], ps[:, 0:w]), inc=nb + "ev")
            bd.wait("sync", nb + "ev")
            bd.dma("sync", out_hbm[128 * mt:128 * (mt + 1), q * 512:q * 512 + w],
                   otile[j][:, 0:w], nb + f"st{j}")
        else:
            bd.emit("vector", lambda e, mt=mt, q=q, ps=ps, w=w: e.tensor_copy(
                out_tiles[mt][:, q * 512:q * 512 + w], ps[:, 0:w]), inc=nb + "ev")

    emit_mm(bd, nb, len(rhs_tiles), 8, T, lhsT, rhs, evac, ps_mm,
            ps_keys=["lnR0", "lnR1"])
    if out_hbm is not None:
        for j in range(4):
            bd.guard_set(f"otile{j}", nb + f"st{j}")
    return


def build_pd1(T):
    nc = bass.Bass("TRN2", target_bir_lowering=False)
    X = nc.dram_tensor("x", [D, T], F32, kind="ExternalInput").ap()
    WQ = nc.dram_tensor("wq", [D, D], F32, kind="ExternalInput").ap()
    WK = nc.dram_tensor("wk", [D, D], F32, kind="ExternalInput").ap()
    WI = nc.dram_tensor("wi", [D, D], F32, kind="ExternalInput").ap()
    IDEN = nc.dram_tensor("iden", [128, 128], F32, kind="ExternalInput").ap()
    QO = nc.dram_tensor("qo", [D, T], BF16, kind="ExternalOutput").ap()
    KO = nc.dram_tensor("ko", [D, T], BF16, kind="ExternalOutput").ap()
    IQO = nc.dram_tensor("iqo", [D, T], BF16, kind="ExternalOutput").ap()
    IKO = nc.dram_tensor("iko", [D, T], BF16, kind="ExternalOutput").ap()
    KS = nc.dram_tensor("ks", [2, D], F32, kind="ExternalOutput").ap()
    es = contextlib.ExitStack()
    with es:
        bd = Bld(nc, es)
        x_sb = [bd.sb(f"x{i}", [128, T], F32) for i in range(8)]
        xn = [bd.sb(f"xn{i}", [128, T], BF16) for i in range(8)]
        setA = [bd.sb(f"wtA{i}", [128, D], BF16) for i in range(8)]
        setB = [bd.sb(f"wtB{i}", [128, D], BF16) for i in range(8)]
        setD = [bd.sb(f"wtD{i}", [128, D], BF16) for i in range(8)]
        nat = [bd.sb(f"nat{j}", [128, D], BF16) for j in range(2)]
        iden = bd.sb("iden_sb", [128, 128], BF16)
        otile = [bd.sb(f"ot{j}", [128, 512], BF16) for j in range(4)]
        xs_col = [bd.sb(f"xsc{i}", [128, 1], BF16) for i in range(8)]
        xs_colf = [bd.sb(f"xscf{i}", [128, 1], F32) for i in range(8)]
        k_colb = [bd.sb(f"kcb{i}", [128, 1], BF16) for i in range(8)]
        k_colf = [bd.sb(f"kcf{i}", [128, 1], F32) for i in range(8)]
        ik_colf = [bd.sb(f"ikcf{i}", [128, 1], F32) for i in range(8)]
        ps_t = [bd.psum(f"pstb{j}", [128, 1024], BF16)[:, 0:128] for j in range(2)]
        ps_mm = [bd.psum(f"psmm{j}", [128, 512], F32) for j in range(2)]
        ps_M2 = [bd.psum(f"psM2{j}", [128, 512], F32) for j in range(2)]
        ps_small = bd.psum("ps_small", [128, 512], F32)
        ps_c = ps_small[:, 480:481]
        ctx = make_ln_ctx(bd, T, ps_R=ps_mm, ps_M=ps_M2, ps_small=ps_small)
        for i in range(8):
            bd.dma("sync", x_sb[i][:, :], X[128 * i:128 * (i + 1), :], "ldx")
        bd.dma("gpsimd", iden[:, :], IDEN[:, :], "iden")
        emit_ln(bd, ctx, x_sb, xn, "L", ready=("ldx", bd.cnt("ldx")))
        bd.wait("vector", "Luse")
        for i in range(8):
            bd.emit("vector", lambda e, i=i: e.tensor_reduce(
                xs_colf[i][:, 0:1], xn[i][:, 0:T], AX.X, ALU.add))
            bd.emit("vector", lambda e, i=i: e.tensor_copy(
                xs_col[i][:, 0:1], xs_colf[i][:, 0:1]),
                inc=("xsc" if i == 7 else None))
        emit_transpose_load(bd, "tq", WQ, D, D, setA, nat, iden, ps_t)
        bd.wait("tensor", "tqte")
        emit_proj_stream(bd, "pq", setA, xn, QO, T, otile, ps_mm)
        emit_transpose_load(bd, "tk", WK, D, D, setB, nat, iden, ps_t)
        bd.wait("tensor", "tkte")
        emit_proj_stream(bd, "pk", setB, xn, KO, T, otile, ps_mm)
        bd.wait("tensor", "xsc")
        bd.guard_wait("tensor", "lnstats")
        for mt in range(8):
            for kt in range(8):
                bd.emit("tensor", lambda e, kt=kt, mt=mt: e.matmul(
                    ps_c[:, 0:1], setB[kt][:, 128 * mt:128 * (mt + 1)],
                    xs_col[kt][:, 0:1], start=(kt == 0), stop=(kt == 7)),
                    inc=("ksmm" if kt == 7 else None))
            bd.wait("vector", "ksmm")
            bd.emit("vector", lambda e, mt=mt: e.tensor_copy(k_colf[mt][:, 0:1], ps_c[:, 0:1]))
            bd.emit("vector", lambda e, mt=mt: e.tensor_copy(k_colb[mt][:, 0:1], ps_c[:, 0:1]),
                    inc="ksev")
            bd.wait("tensor", "ksev")
        bd.guard_set("lnstats", "ksev")
        emit_transpose_load(bd, "ti", WI, D, D, setD, nat, iden, ps_t)
        bd.wait("tensor", "tite")
        bd.guard_wait("tensor", "lnstats")
        for mt in range(8):
            for kt in range(8):
                bd.emit("tensor", lambda e, kt=kt, mt=mt: e.matmul(
                    ps_c[:, 0:1], setD[kt][:, 128 * mt:128 * (mt + 1)],
                    k_colb[kt][:, 0:1], start=(kt == 0), stop=(kt == 7)),
                    inc=("iksmm" if kt == 7 else None))
            bd.wait("vector", "iksmm")
            bd.emit("vector", lambda e, mt=mt: e.tensor_copy(ik_colf[mt][:, 0:1], ps_c[:, 0:1]),
                    inc="iksev")
            bd.wait("tensor", "iksev")
        bd.guard_set("lnstats", "iksev")
        bd.wait("gpsimd", "pqev")
        for i in range(8):
            bd.dma("gpsimd", setA[i][:, :], WQ[128 * i:128 * (i + 1), :], "natq")

        def wt_product(nb, nat_set, out_set):
            def lhsT(kt, mt):
                return nat_set[kt][:, 128 * mt:128 * (mt + 1)]
            def rhs(kt, q, w):
                return setD[kt][:, q * 512:q * 512 + w]
            def evac(mt, q, w, ps):
                bd.wait("vector", nb + "mm")
                bd.emit("vector", lambda e, mt=mt, q=q, ps=ps, w=w: e.tensor_copy(
                    out_set[mt][:, q * 512:q * 512 + w], ps[:, 0:w]), inc=nb + "ev")
            emit_mm(bd, nb, 8, 8, D, lhsT, rhs, evac, ps_mm,
                    ps_keys=["lnR0", "lnR1"])

        bd.wait("tensor", "natq")
        wt_product("wq2", setA, setB)
        bd.wait("tensor", "wq2ev")
        emit_proj_stream(bd, "piq", setB, xn, IQO, T, otile, ps_mm)
        bd.wait("gpsimd", "piqev")
        for i in range(8):
            bd.dma("gpsimd", setA[i][:, :], WK[128 * i:128 * (i + 1), :], "natk")
        bd.wait("tensor", "natk")
        wt_product("wk2", setA, setB)
        bd.wait("tensor", "wk2ev")
        emit_proj_stream(bd, "pik", setB, xn, IKO, T, otile, ps_mm)
        bd.wait("sync", "ksev")
        bd.wait("sync", "iksev")
        for i in range(8):
            bd.dma("sync", KS[0:1, 128 * i:128 * (i + 1)], k_colf[i][:, 0:1], "kst")
            bd.dma("sync", KS[1:2, 128 * i:128 * (i + 1)], ik_colf[i][:, 0:1], "kst")
        for p in ("pq", "pk", "piq", "pik"):
            for j in range(4):
                bd.wait("sync", p + f"st{j}")
        bd.wait("sync", "kst")
        bd.finalize()
    return nc


def build_pd2(T):
    nc = bass.Bass("TRN2", target_bir_lowering=False)
    ap = lambda n, s, d=F32, k="ExternalInput": nc.dram_tensor(n, s, d, kind=k).ap()
    X = ap("x", [D, T])
    QH = ap("qh", [D, T], BF16); KH = ap("kh", [D, T], BF16)
    IQH = ap("iqh", [D, T], BF16); IKH = ap("ikh", [D, T], BF16)
    PREVS = ap("prevs", [16, D])
    PMK = ap("pmk", [16, 1]); PMI = ap("pmi", [16, 1])
    VF = ap("vf", [8, D, 64]); WF = ap("wf", [8, D, 64]); UF = ap("uf", [8, D, 64])
    BF = ap("bfw", [8, D])
    VI = ap("vi", [8, D, 64]); WIE = ap("wie", [8, D, 64]); UI = ap("ui", [8, D, 64])
    BI = ap("bi", [8, D])
    RW1 = ap("rw1", [D, D]); RB1 = ap("rb1", [D, 1])
    RW2 = ap("rw2", [8, D]); RB2 = ap("rb2", [8, 1]); EB = ap("eb", [8, 1])
    ALPHA = ap("alpha", [128, 1]); NV = ap("nv", [1, T])
    REP = ap("rep", [8, 512]); IDEN = ap("iden", [128, 128])
    WO = ap("wo", [D, D])
    OUT = ap("out", [D, T], F32, k="ExternalOutput")
    NQ = cdiv(T, 512)
    es = contextlib.ExitStack()
    with es:
        bd = Bld(nc, es)
        ks_t = [bd.sb(f"kst{i}", [128, T], BF16) for i in range(8)]
        cum = [bd.sb(f"cum{k}", [128, T], F32) for k in range(4)]
        acc = [bd.sb(f"acc{i}", [128, T], BF16) for i in range(8)]
        prod = [bd.sb(f"prod{k}", [128, T], BF16) for k in range(4)]
        rw1t = [bd.sb(f"rw1t{i}", [128, D], BF16) for i in range(8)]
        yw = bd.sb("yw", [128, T], F32)
        ones_f = bd.sb("ones_f", [128, T], F32)
        rnb = bd.sb("rnb", [128, T], F32)
        wflat = [bd.sb(f"wfl{i}", [128, 512], BF16) for i in range(8)]
        vflat = [bd.sb(f"vfl{i}", [128, 512], BF16) for i in range(8)]
        unat = [bd.sb(f"unat{i}", [128, 512], BF16) for i in range(8)]
        ut = [bd.sb(f"ut{k}", [128, D], BF16) for k in range(4)]
        hbf = bd.sb("hbf", [128, 512], BF16)
        lg8 = bd.sb("lg8", [8, 512], F32)
        iden_f = bd.sb("iden_f", [128, 128], F32)
        rbeb_c = bd.sb("rbeb_c", [8, 1], F32)
        w8t = bd.sb("w8t", [8, T], BF16)
        wtokf = bd.sb("wtokf", [128, 128], F32)
        wtokb = bd.sb("wtokb", [128, 128], BF16)
        nat = [bd.sb(f"nat{j}", [128, D], BF16) for j in range(2)]
        iden = bd.sb("iden_sb", [128, 128], BF16)
        repsb = bd.sb("repsb", [8, 512], BF16)
        bnat = bd.sb("bnat", [8, D], BF16)
        prevsb = bd.sb("prevsb", [16, D], F32)
        prevmb = bd.sb("prevmb", [16, D], BF16)
        pmcol = bd.sb("pmcol", [16, 1], F32)
        o8col = bd.sb("o8col", [16, 1], BF16)
        rb1c = [bd.sb(f"rb1c{i}", [128, 1], F32) for i in range(8)]
        rb2_sb = bd.sb("rb2_sb", [8, 1], F32)
        eb_sb = bd.sb("eb_sb", [8, 1], F32)
        alpha_c = bd.sb("alpha_c", [128, 1], F32)
        nvrow = bd.sb("nvrow", [1, T], F32)
        rnrow = bd.sb("rnrow", [1, T], F32)
        ones_r = bd.sb("ones_r", [1, 128], F32)
        pkcol = [bd.sb(f"pkc{i}", [128, 1], BF16) for i in range(8)]
        offc = [bd.sb(f"offc{k}", [128, 1], F32) for k in range(4)]
        nmx = bd.sb("nmx", [128, 1], F32)
        dencol = bd.sb("dencol", [128, 1], F32)
        rden = bd.sb("rden", [128, 1], F32)
        xt = [bd.sb(f"xt{j}", [128, 512], F32) for j in range(2)]
        tso = bd.sb("tso", [128, 512], F32)
        otile = [bd.sb(f"ot{j}", [128, 512], F32) for j in range(4)]
        wto = [bd.sb(f"wto{i}", [128, D], BF16) for i in range(8)]
        ps_mm = [bd.psum(f"psmm{j}", [128, 512], F32) for j in range(2)]
        ps_w = [bd.psum(f"psw{j}", [128, 512], F32) for j in range(2)]
        ps_lg = bd.psum("pslg", [128, 512], F32)
        ps_t = [bd.psum(f"pstb{j}", [128, 1024], BF16)[:, 0:128] for j in range(2)]
        ps_c = bd.psum("psc", [128, 512], F32)

        bd.dma("gpsimd", iden[:, :], IDEN[:, :], "iden")
        bd.dma("sync", iden_f[:, :], IDEN[:, :], "iden2")
        bd.dma("gpsimd", repsb[:, :], REP[:, :], "cst")
        for i in range(8):
            bd.dma("sync", rb1c[i][:, :], RB1[128 * i:128 * (i + 1), :], "cstf")
        bd.dma("sync", rb2_sb[:, :], RB2[:, :], "cstf")
        bd.dma("sync", eb_sb[:, :], EB[:, :], "cstf")
        bd.dma("sync", alpha_c[:, :], ALPHA[:, :], "cstf")
        bd.dma("sync", nvrow[:, :], NV[:, :], "cstf")
        bd.dma("sync", prevsb[:, :], PREVS[:, :], "cstf")
        bd.wait("gpsimd", "cstf")
        bd.wait("gpsimd", "iden")
        bd.wait("tensor", "iden2")
        bd.wait("tensor", "cst")
        bd.wait("tensor", "cstf")
        bd.emit("gpsimd", lambda e: e.memset(ones_f[:, :], 1.0))
        bd.emit("gpsimd", lambda e: e.memset(ones_r[:, :], 1.0))
        t_o8 = bd.emit("gpsimd", lambda e: e.memset(o8col[:, :], 1.0))
        bd.wait("vector", "cstf")
        bd.emit("vector", lambda e: e.tensor_add(rbeb_c[0:8, 0:1], rb2_sb[0:8, 0:1], eb_sb[0:8, 0:1]))
        t_rn = bd.emit("vector", lambda e: e.reciprocal(rnrow[0:1, 0:T], nvrow[0:1, 0:T]))
        bd.wait_tok("tensor", t_rn)
        bd.wait_tok("tensor", bd.tok("gpsimd"))
        for q in range(NQ):
            w = min(512, T - q * 512)
            bd.guard_wait("tensor", "psc")
            bd.emit("tensor", lambda e, q=q, w=w: e.matmul(
                ps_c[:, 0:w], ones_r[:, :], rnrow[0:1, q * 512:q * 512 + w],
                start=True, stop=True), inc="rnmm")
            bd.wait("vector", "rnmm")
            tk = bd.emit("vector", lambda e, q=q, w=w: e.tensor_copy(
                rnb[:, q * 512:q * 512 + w], ps_c[:, 0:w]), inc="rnev")
            bd.guard_set("psc", "rnev")
        rw2t = [bd.sb(f"rw2t{i}", [128, 8], BF16) for i in range(8)]
        emit_transpose_load(bd, "tr2", RW2, 8, D, rw2t, nat, iden, ps_t)
        emit_transpose_load(bd, "tr1", RW1, D, D, rw1t, nat, iden, ps_t)

        def branch(br):
            inv = (br == "i")
            Ksrc = IKH if inv else KH
            Qsrc = IQH if inv else QH
            Wsrc = VI if inv else WF
            Vsrc = WIE if inv else VF
            Usrc = UI if inv else UF
            Bsrc = BI if inv else BF
            PM = PMI if inv else PMK
            nb = br
            if inv:
                for qeng in ("gpsimd", "sync"):
                    bd.wait(qeng, "fom")
                    bd.wait_tok(qeng, bd.tok("vector"))
                    bd.wait_tok(qeng, bd.tok("scalar"))
            for i in range(8):
                bd.dma("gpsimd", wflat[i][:, :].rearrange("p (k r) -> p k r", k=8),
                       Wsrc.rearrange("k d r -> d k r")[128 * i:128 * (i + 1), :, :],
                       nb + "wfl")
            for i in range(8):
                bd.dma("sync", ks_t[i][:, :], Ksrc[128 * i:128 * (i + 1), :], nb + "k")
            bd.dma("sync", pmcol[:, :], PM[:, :], nb + "pm")
            bd.wait("vector", nb + "pm")
            t_pm = bd.emit("vector", lambda e: e.tensor_scalar_mul(
                prevmb[0:16, :], prevsb[0:16, :], pmcol[0:16, 0:1]))
            bd.wait_tok("tensor", t_pm)
            bd.wait_tok("tensor", t_o8)
            for i in range(8):
                bd.guard_wait("tensor", "psc")
                bd.emit("tensor", lambda e, i=i: e.matmul(
                    ps_c[:, 0:1], prevmb[0:16, 128 * i:128 * (i + 1)], o8col[0:16, 0:1],
                    start=True, stop=True), inc=nb + "pkm")
                bd.wait("vector", nb + "pkm")
                bd.emit("vector", lambda e, i=i: e.tensor_copy(
                    pkcol[i][:, 0:1], ps_c[:, 0:1]), inc=nb + "pkev")
                bd.guard_set("psc", nb + "pkev")
            bd.wait("tensor", nb + "wfl")
            for k in range(4):
                bd.wait("tensor", nb + "pkev", 8)
                bd.guard_wait("tensor", "psc")
                for i in range(8):
                    bd.emit("tensor", lambda e, i=i, k=k: e.matmul(
                        ps_c[:, 0:1], wflat[i][:, 128 * k:128 * (k + 1)],
                        pkcol[i][:, 0:1], start=(i == 0), stop=(i == 7)),
                        inc=(nb + "ofm") if i == 7 else None)
                bd.wait("vector", nb + "ofm")
                bd.emit("vector", lambda e, k=k: e.tensor_copy(
                    offc[k][:, 0:1], ps_c[:, 0:1]), inc=nb + "ofev")
                bd.guard_set("psc", nb + "ofev")
            bd.wait("tensor", nb + "k")
            for k in range(4):
                for q in range(NQ):
                    w = min(512, T - q * 512)
                    ps = ps_mm[q % 2]
                    if q < 2:
                        bd.guard_wait("tensor", f"psmm{q % 2}")
                    else:
                        bd.wait("tensor", nb + f"ywev{k}", q - 1)
                    for i in range(8):
                        bd.emit("tensor", lambda e, i=i, k=k, q=q, w=w, ps=ps: e.matmul(
                            ps[:, 0:w], wflat[i][:, 128 * k:128 * (k + 1)],
                            ks_t[i][:, q * 512:q * 512 + w],
                            start=(i == 0), stop=(i == 7)),
                            inc=(nb + f"ywmm{k}") if i == 7 else None)
                    bd.wait("vector", nb + f"ywmm{k}")
                    bd.emit("vector", lambda e, k=k, q=q, w=w, ps=ps: e.tensor_copy(
                        yw[:, q * 512:q * 512 + w], ps[:, 0:w]), inc=nb + f"ywev{k}")
                bd.guard_set(f"psmm{0}", nb + f"ywev{k}")
                bd.guard_set(f"psmm{1}", nb + f"ywev{k}")
                bd.emit("vector", lambda e, k=k: e.tensor_tensor_scan(
                    cum[k][:, 0:T], ones_f[:, 0:T], yw[:, 0:T], offc[k][:, 0:1],
                    ALU.mult, ALU.add), inc=nb + "scan")
            bd.wait("sync", nb + "ywmm3")
            for i in range(8):
                bd.dma("sync", ks_t[i][:, :], Qsrc[128 * i:128 * (i + 1), :], nb + "q")
            for i in range(8):
                bd.dma("gpsimd", vflat[i][:, :].rearrange("p (k r) -> p k r", k=8),
                       Vsrc.rearrange("k d r -> d k r")[128 * i:128 * (i + 1), :, :],
                       nb + "vfl")
            for i in range(8):
                bd.dma("gpsimd", unat[i][:, :].rearrange("p (k r) -> p k r", k=8),
                       Usrc.rearrange("k d r -> d k r")[128 * i:128 * (i + 1), :, :],
                       nb + "ufl")
            bd.dma("gpsimd", bnat[:, :], Bsrc[:, :], nb + "bfl")
            bd.wait("tensor", nb + "ufl")
            tcnt = 0
            for i in range(8):
                for k in range(4):
                    ps = ps_t[tcnt % 2]
                    if tcnt < 2:
                        bd.guard_wait("tensor", f"pstr{tcnt % 2}")
                    else:
                        bd.wait("tensor", nb + "ute", tcnt - 1)

                    def _tr(e, ps=ps, i=i, k=k):
                        with bd.nc.allow_low_precision(reason="bf16 PE transpose"):
                            return e.transpose(ps[0:128, 0:128],
                                               unat[i][:, 128 * k:128 * (k + 1)],
                                               iden[:, :])
                    bd.emit("tensor", _tr, inc=nb + "utp")
                    bd.wait("vector", nb + "utp")
                    bd.emit("vector", lambda e, ps=ps, i=i, k=k: e.tensor_copy(
                        ut[k][:, 128 * i:128 * (i + 1)], ps[0:128, 0:128]),
                        inc=nb + "ute")
                    tcnt += 1
            bd.guard_set("pstr0", nb + "ute")
            bd.guard_set("pstr1", nb + "ute")
            bd.wait("tensor", nb + "q")
            bd.wait("tensor", "tr2te")
            for q in range(NQ):
                w = min(512, T - q * 512)
                bd.guard_wait("tensor", "pslg")
                for rht in range(8):
                    ps = ps_mm[rht % 2]
                    if rht < 2:
                        bd.guard_wait("tensor", f"psmm{rht % 2}")
                    else:
                        bd.wait("tensor", nb + f"hg{q}", rht - 1)
                    for i in range(8):
                        bd.emit("tensor", lambda e, i=i, rht=rht, q=q, w=w, ps=ps: e.matmul(
                            ps[:, 0:w], rw1t[i][:, 128 * rht:128 * (rht + 1)],
                            ks_t[i][:, q * 512:q * 512 + w],
                            start=(i == 0), stop=(i == 7)),
                            inc=(nb + f"hmm{q}") if i == 7 else None)
                    bd.wait("scalar", nb + f"hmm{q}")
                    bd.emit("scalar", lambda e, rht=rht, q=q, w=w, ps=ps: e.activation(
                        hbf[:, 0:w], ps[:, 0:w], AF.Gelu,
                        bias=rb1c[rht][:, 0:1]), inc=nb + f"hg{q}")
                    bd.wait("tensor", nb + f"hg{q}")
                    bd.emit("tensor", lambda e, rht=rht, q=q, w=w: e.matmul(
                        ps_lg[0:8, 0:w], rw2t[rht][:, 0:8], hbf[:, 0:w],
                        start=(rht == 0), stop=(rht == 7)),
                        inc=(nb + "lgm") if rht == 7 else None)
                bd.guard_set("psmm0", nb + f"hg{q}", bd.cnt(nb + f"hg{q}"))
                bd.guard_set("psmm1", nb + f"hg{q}", bd.cnt(nb + f"hg{q}"))
                bd.wait("vector", nb + "lgm")
                t_lg = bd.emit("vector", lambda e, q=q, w=w: e.tensor_scalar_add(
                    lg8[0:8, 0:w], ps_lg[0:8, 0:w], rbeb_c[0:8, 0:1]))
                bd.guard_set("pslg", "self_vector", t_lg[1])
                for tt in range(w // 128):
                    bd.wait_tok("tensor", t_lg)
                    bd.guard_wait("tensor", "psc")
                    bd.emit("tensor", lambda e, tt=tt: e.transpose(
                        ps_c[0:128, 0:8], lg8[0:8, 128 * tt:128 * (tt + 1)],
                        iden_f[0:8, 0:8]), inc=nb + "ltr")
                    bd.wait("vector", nb + "ltr")
                    t1 = bd.emit("vector", lambda e: e.reduce_max(
                        nmx[:, 0:1], ps_c[0:128, 0:8], AX.X, negate=True))
                    bd.wait_tok("scalar", t1)
                    t2 = bd.emit("scalar", lambda e: e.activation(
                        wtokf[:, 0:8], ps_c[0:128, 0:8], AF.Exp,
                        bias=nmx[:, 0:1], accum_out=dencol[:, 0:1]))
                    bd.guard_set("psc", "self_scalar", t2[1])
                    bd.wait_tok("vector", t2)
                    t3 = bd.emit("vector", lambda e: e.reciprocal(rden[:, 0:1], dencol[:, 0:1]))
                    t4 = bd.emit("vector", lambda e: e.tensor_scalar_mul(
                        wtokb[:, 0:8], wtokf[:, 0:8], rden[:, 0:1]), inc=nb + "wt")
                    bd.wait("tensor", nb + "wt")
                    pst = ps_t[tt % 2]
                    bd.guard_wait("tensor", f"pstr{tt % 2}")

                    def _trw(e, pst=pst):
                        with bd.nc.allow_low_precision(reason="bf16 PE transpose"):
                            return e.transpose(pst[0:8, 0:128], wtokb[:, 0:8],
                                               iden[0:128, 0:128])
                    bd.emit("tensor", _trw, inc=nb + "w8p")
                    bd.wait("vector", nb + "w8p")
                    tw = bd.emit("vector", lambda e, pst=pst, q=q, tt=tt: e.tensor_copy(
                        w8t[0:8, q * 512 + 128 * tt:q * 512 + 128 * (tt + 1)],
                        pst[0:8, 0:128]), inc=nb + "w8e")
                    bd.guard_set(f"pstr{tt % 2}", "self_vector", tw[1])
            bd.wait("tensor", nb + "vfl")
            bd.wait("tensor", nb + "w8e")
            bd.wait("tensor", nb + "scan")
            for k in range(4):
                for q in range(NQ):
                    w = min(512, T - q * 512)
                    psx = ps_mm[0]
                    psw_ = ps_w[0]
                    bd.guard_wait("tensor", "psmm0")
                    bd.guard_wait("tensor", "psw0")
                    for i in range(8):
                        bd.emit("tensor", lambda e, i=i, k=k, q=q, w=w, psx=psx: e.matmul(
                            psx[:, 0:w], vflat[i][:, 128 * k:128 * (k + 1)],
                            ks_t[i][:, q * 512:q * 512 + w],
                            start=(i == 0), stop=(i == 7)),
                            inc=(nb + "xvm") if i == 7 else None)
                    bd.emit("tensor", lambda e, k=k, q=q, w=w, psw_=psw_: e.matmul(
                        psw_[:, 0:w], repsb[:, 128 * k:128 * (k + 1)],
                        w8t[0:8, q * 512:q * 512 + w], start=True, stop=True),
                        inc=nb + "wxm")
                    bd.wait("vector", nb + "xvm")
                    t5 = bd.emit("vector", lambda e, k=k, q=q, w=w, psx=psx: e.tensor_mul(
                        tso[:, 0:w], psx[:, 0:w], cum[k][:, q * 512:q * 512 + w]))
                    bd.guard_set("psmm0", "self_vector", t5[1])
                    bd.wait("vector", nb + "wxm")
                    t6 = bd.emit("vector", lambda e, k=k, q=q, w=w, psw_=psw_: e.tensor_mul(
                        prod[k][:, q * 512:q * 512 + w], tso[:, 0:w], psw_[:, 0:w]),
                        inc=nb + "prod")
                    bd.guard_set("psw0", "self_vector", t6[1])
            bd.wait("tensor", nb + "prod")
            bd.wait("tensor", nb + "bfl")
            for i in range(8):
                for q in range(NQ):
                    w = min(512, T - q * 512)
                    ps = ps_mm[1]
                    bd.guard_wait("tensor", "psmm1")
                    for k in range(4):
                        bd.emit("tensor", lambda e, i=i, k=k, q=q, w=w, ps=ps: e.matmul(
                            ps[:, 0:w], ut[k][:, 128 * i:128 * (i + 1)],
                            prod[k][:, q * 512:q * 512 + w],
                            start=(k == 0), stop=False))
                    bd.emit("tensor", lambda e, i=i, q=q, w=w, ps=ps: e.matmul(
                        ps[:, 0:w], bnat[0:8, 128 * i:128 * (i + 1)],
                        w8t[0:8, q * 512:q * 512 + w], start=False, stop=True),
                        inc=nb + "om")
                    bd.wait("vector", nb + "om")
                    if not inv:
                        t7 = bd.emit("vector", lambda e, i=i, q=q, w=w, ps=ps: e.tensor_copy(
                            acc[i][:, q * 512:q * 512 + w], ps[:, 0:w]), inc=nb + "oe")
                    else:
                        t7 = bd.emit("vector", lambda e, i=i, q=q, w=w, ps=ps:
                                     e.scalar_tensor_tensor(
                                         acc[i][:, q * 512:q * 512 + w], ps[:, 0:w],
                                         alpha_c[:, 0:1],
                                         acc[i][:, q * 512:q * 512 + w],
                                         ALU.mult, ALU.add), inc=nb + "oe")
                    bd.guard_set("psmm1", "self_vector", t7[1])

        branch("f")
        branch("i")
        emit_transpose_load(bd, "two", WO, D, D, wto, nat, iden, ps_t)
        bd.wait("tensor", "twote")
        bd.wait("tensor", "ioe")
        ti = 0
        for i in range(8):
            for q in range(NQ):
                w = min(512, T - q * 512)
                ps = ps_mm[ti % 2]
                if ti < 2:
                    bd.guard_wait("tensor", f"psmm{ti % 2}")
                else:
                    bd.wait("tensor", "foev", ti - 1)
                for k in range(8):
                    bd.emit("tensor", lambda e, i=i, k=k, q=q, w=w, ps=ps: e.matmul(
                        ps[:, 0:w], wto[k][:, 128 * i:128 * (i + 1)],
                        acc[k][:, q * 512:q * 512 + w],
                        start=(k == 0), stop=(k == 7)),
                        inc=("fom") if k == 7 else None)
                jx = ti % 2
                bd.dma("sync", xt[jx][:, 0:w], X[128 * i:128 * (i + 1), q * 512:q * 512 + w],
                       f"xl{jx}")
                bd.wait("vector", "fom")
                bd.wait("vector", f"xl{jx}")
                t8 = bd.emit("vector", lambda e, q=q, w=w, ps=ps: e.tensor_mul(
                    tso[:, 0:w], ps[:, 0:w], rnb[:, q * 512:q * 512 + w]))
                bd.guard_set(f"psmm{ti % 2}", "self_vector", t8[1])
                bd.emit("vector", lambda e: e.engine_nop(), inc="foev")
                jo = ti % 4
                if ti >= 4:
                    bd.wait("vector", f"ost{jo}", 16 * (ti // 4))
                t9 = bd.emit("vector", lambda e, jx=jx, jo=jo, w=w: e.scalar_tensor_tensor(
                    otile[jo][:, 0:w], tso[:, 0:w], 0.5, xt[jx][:, 0:w],
                    ALU.mult, ALU.add))
                bd.wait_tok("sync", t9)
                bd.dma("sync", OUT[128 * i:128 * (i + 1), q * 512:q * 512 + w],
                       otile[jo][:, 0:w], f"ost{jo}")
                ti += 1
        for j in range(4):
            bd.wait("sync", f"ost{j}")
        bd.finalize()
    return nc


def build_pe1(T):
    nc = bass.Bass("TRN2", target_bir_lowering=False)
    ap = lambda n, s, d=F32, k="ExternalInput": nc.dram_tensor(n, s, d, kind=k).ap()
    X = ap("x", [D, T])
    WQKV = ap("wqkv", [D, 3 * D])
    PROJ = ap("proj", [16, 64, 256])
    QPO = ap("qpo", [4096, T], BF16, k="ExternalOutput")
    KVO = ap("kvo", [4096, 65], F32, k="ExternalOutput")
    NQ = cdiv(T, 512)
    es = contextlib.ExitStack()
    with es:
        bd = Bld(nc, es)
        x_sb = [bd.sb(f"x{i}", [128, T], F32) for i in range(8)]
        xn = [bd.sb(f"xn{i}", [128, T], BF16) for i in range(8)]
        wslot = [bd.sb(f"wsl{i}", [128, D], BF16) for i in range(8)]
        projsb = [bd.sb(f"proj{g}", [128, 256], BF16) for g in range(8)]
        qk_t = [bd.sb(f"qkt{j}", [128, T], BF16) for j in range(2)]
        vext = [bd.sb(f"vext{i}", [128, 16 * 65], BF16) for i in range(cdiv(2048, 128))]
        kp = bd.sb("kp", [128, 256 * (T // 128)], BF16)
        qp_o = [bd.sb(f"qpo{j}", [128, 512], BF16) for j in range(4)]
        esb = bd.sb("esb", [128, 512], F32)
        kvo_sb = [bd.sb(f"kvo{j}", [128, 65], F32) for j in range(4)]
        ps_mm = [bd.psum(f"psmm{j}", [128, 512], F32) for j in range(2)]
        ps_kv = [bd.psum(f"pskv{j}", [128, 65], F32) for j in range(2)]
        rot = PsRot(bd, ps_mm, ["lnR0", "lnR1"])
        ctx = make_ln_ctx(bd, T, ps_R=ps_mm, ps_M=None, ps_small=None)
        for i in range(8):
            bd.dma("sync", x_sb[i][:, :], X[128 * i:128 * (i + 1), :], "ldx")
        for g in range(8):
            bd.dma("gpsimd", projsb[g][0:64, :], PROJ[2 * g, :, :], "ldp")
            bd.dma("gpsimd", projsb[g][64:128, :], PROJ[2 * g + 1, :, :], "ldp")
        emit_ln(bd, ctx, x_sb, xn, "L", ready=("ldx", bd.cnt("ldx")))
        NT = T // 128

        for i in range(8):
            bd.dma("gpsimd", wslot[i][:, :], WQKV[128 * i:128 * (i + 1), 2 * D:3 * D], "wv")
        bd.wait("gpsimd", "Luse")
        for tt in range(NT):
            bd.emit("gpsimd", lambda e, tt=tt: e.memset(
                vext[tt][:, :].rearrange("p (h c) -> p h c", h=16)[:, :, 64:65], 1.0),
                inc="vone")
        bd.wait("tensor", "wv")
        bd.wait("tensor", "Luse")
        for tt in range(NT):
            for half in range(2):
                ps = rot.acquire()
                for i in range(8):
                    bd.emit("tensor", lambda e, i=i, tt=tt, half=half, ps=ps: e.matmul(
                        ps[:, 0:512], xn[i][:, 128 * tt:128 * (tt + 1)],
                        wslot[i][:, 512 * half:512 * (half + 1)],
                        start=(i == 0), stop=(i == 7)),
                        inc=("vmm") if i == 7 else None)
                bd.wait("vector", "vmm")
                hbase = half * 8
                tv = bd.emit("vector", lambda e, tt=tt, hbase=hbase, ps=ps: e.tensor_copy(
                    vext[tt][:, :].rearrange("p (h c) -> p h c", h=16)[:, hbase:hbase + 8, 0:64],
                    ps[:, 0:512].rearrange("p (h c) -> p h c", h=8)),
                    inc="vev")
                rot.release(tv)

        for part, wofs, name in (("q", 0, "qp"), ("k", D, "kp")):
            bd.wait("gpsimd", "vmm" if part == "q" else "qmm")
            for i in range(8):
                bd.dma("gpsimd", wslot[i][:, :], WQKV[128 * i:128 * (i + 1), wofs:wofs + D],
                       part + "wl")
            bd.wait("tensor", part + "wl")
            for g in range(8):
                jj = g % 2
                qk = qk_t[jj]
                for q in range(NQ):
                    w = min(512, T - q * 512)
                    ps = rot.acquire()
                    for i in range(8):
                        bd.emit("tensor", lambda e, i=i, g=g, q=q, w=w, ps=ps: e.matmul(
                            ps[:, 0:w], wslot[i][:, 128 * g:128 * (g + 1)],
                            xn[i][:, q * 512:q * 512 + w],
                            start=(i == 0), stop=(i == 7)),
                            inc=(part + "mm") if i == 7 else None)
                    bd.wait("vector", part + "mm")
                    if g >= 2 and q == 0:
                        bd.wait("vector", part + "done", g - 1)
                    tv = bd.emit("vector", lambda e, jj=jj, q=q, w=w, ps=ps: e.tensor_copy(
                        qk_t[jj][:, q * 512:q * 512 + w], ps[:, 0:w]), inc=part + "ev")
                    rot.release(tv)
                if part == "q":
                    emit_pe1_qproj(bd, g, qk, projsb, qp_o, esb, rot, QPO, T)
                else:
                    emit_pe1_kside(bd, g, qk, projsb, kp, esb, vext, rot, ps_kv,
                                   kvo_sb, KVO, T)
        for j in range(4):
            bd.wait("sync", f"qpst{j}")
            bd.wait("sync", f"kvst{j}")
        bd.finalize()
    return nc


def emit_pe1_qproj(bd, g, qk, projsb, qp_o, esb, rot, QPO, T):
    NQ = cdiv(T, 512)
    bd.wait("tensor", "qev")
    bd.wait("tensor", "ldp")
    for hh in range(2):
        h = 2 * g + hh
        for fh in range(2):
            for q in range(NQ):
                w = min(512, T - q * 512)
                ti = ((g * 2 + hh) * 2 + fh) * NQ + q
                ps = rot.acquire()
                for _ in (0,):
                    bd.emit("tensor", lambda e, g=g, hh=hh, fh=fh, q=q, w=w, ps=ps: e.matmul(
                        ps[:, 0:w], projsb[g][64 * hh:64 * (hh + 1), 128 * fh:128 * (fh + 1)],
                        qk[64 * hh:64 * (hh + 1), q * 512:q * 512 + w],
                        start=True, stop=True), inc="qpmm")
                bd.wait("vector", "qpmm")
                t1 = bd.emit("vector", lambda e, w=w, ps=ps: e.tensor_scalar_min(
                    esb[:, 0:w], ps[:, 0:w], 0.0))
                bd.wait_tok("scalar", t1)
                t2 = bd.emit("scalar", lambda e, w=w: e.activation(
                    esb[:, 0:w], esb[:, 0:w], AF.Exp))
                bd.wait_tok("vector", t2)
                jo = ti % 4
                if ti >= 4:
                    bd.wait("vector", f"qpst{jo}", 16 * (ti // 4))
                t3 = bd.emit("vector", lambda e, w=w, ps=ps, jo=jo: e.scalar_tensor_tensor(
                    qp_o[jo][:, 0:w], ps[:, 0:w], 0.0, esb[:, 0:w], ALU.max, ALU.add),
                    inc="qprev")
                rot.release(t3)
                bd.wait_tok("sync", t3)
                bd.dma("sync", QPO[256 * h + 128 * fh:256 * h + 128 * (fh + 1),
                                   512 * q:512 * q + w], qp_o[jo][:, 0:w], f"qpst{jo}")
    bd.emit("vector", lambda e: e.engine_nop(), inc="qdone")


def emit_pe1_kside(bd, g, qk, projsb, kp, esb, vext, rot, ps_kv, kvo_sb, KVO, T):
    NT = T // 128
    bd.wait("tensor", "kev")
    bd.wait("tensor", "vev")
    bd.wait("tensor", "vone")
    bd.wait("tensor", "ldp")
    for hh in range(2):
        h = 2 * g + hh
        kpv = kp[:, 0:NT * 256].rearrange("p (t f) -> p t f", t=NT)
        for tt in range(NT):
            ps = rot.acquire()
            bd.emit("tensor", lambda e, g=g, hh=hh, tt=tt, ps=ps: e.matmul(
                ps[:, 0:256], qk[64 * hh:64 * (hh + 1), 128 * tt:128 * (tt + 1)],
                projsb[g][64 * hh:64 * (hh + 1), 0:256], start=True, stop=True), inc="kpmm")
            bd.wait("vector", "kpmm")
            t1 = bd.emit("vector", lambda e, ps=ps: e.tensor_scalar_min(
                esb[:, 0:256], ps[:, 0:256], 0.0))
            bd.wait_tok("scalar", t1)
            t2 = bd.emit("scalar", lambda e: e.activation(
                esb[:, 0:256], esb[:, 0:256], AF.Exp))
            bd.wait_tok("vector", t2)
            t3 = bd.emit("vector", lambda e, ps=ps, tt=tt: e.scalar_tensor_tensor(
                kpv[:, tt, 0:256], ps[:, 0:256], 0.0, esb[:, 0:256], ALU.max, ALU.add),
                inc="kpev")
            rot.release(t3)
        bd.wait("tensor", "kpev")
        for fh in range(2):
            ps = ps_kv[fh]
            bd.guard_wait("tensor", f"pskv{fh}")
            for tt in range(NT):
                bd.emit("tensor", lambda e, h=h, fh=fh, tt=tt, ps=ps: e.matmul(
                    ps[:, 0:65], kpv[:, tt, 128 * fh:128 * (fh + 1)],
                    vext[tt][:, :].rearrange("p (hx c) -> p hx c", hx=16)[:, h, 0:65],
                    start=(tt == 0), stop=(tt == NT - 1)),
                    inc=("kvmm") if tt == NT - 1 else None)
            bd.wait("vector", "kvmm")
            jv = (h * 2 + fh) % 4
            ti = h * 2 + fh
            if ti >= 4:
                bd.wait("vector", f"kvst{jv}", 16 * (ti // 4))
            tv = bd.emit("vector", lambda e, jv=jv, ps=ps: e.tensor_copy(
                kvo_sb[jv][:, 0:65], ps[:, 0:65]), inc="kvev")
            bd.guard_set(f"pskv{fh}", "self_vector", tv[1])
            bd.wait_tok("sync", tv)
            bd.dma("sync", KVO[256 * h + 128 * fh:256 * h + 128 * (fh + 1), 0:65],
                   kvo_sb[jv][:, 0:65], f"kvst{jv}")
    bd.emit("vector", lambda e: e.engine_nop(), inc="kdone")


def build_pe2(T):
    nc = bass.Bass("TRN2", target_bir_lowering=False)
    ap = lambda n, s, d=F32, k="ExternalInput": nc.dram_tensor(n, s, d, kind=k).ap()
    X = ap("x", [D, T])
    QP = ap("qp", [4096, T], BF16)
    KVP = ap("kvp", [8, 4096, 65])
    WOUT = ap("wout", [D, D])
    BOUT = ap("bout", [D, 1])
    OUT = ap("out", [D, T], F32, k="ExternalOutput")
    NQ = cdiv(T, 512)
    es = contextlib.ExitStack()
    with es:
        bd = Bld(nc, es)
        kvp_sb = [bd.sb(f"kvp{c}", [128, 32 * 65], F32) for c in range(8)]
        kv_bf = [bd.sb(f"kvb{m}", [128, 65], BF16) for m in range(32)]
        qp_sb = [bd.sb(f"qps{j}", [128, T], BF16) for j in range(4)]
        att = [bd.sb(f"att{g}", [128, T], BF16) for g in range(8)]
        wosb = [bd.sb(f"wo{i}", [128, D], BF16) for i in range(8)]
        bout_c = [bd.sb(f"boc{i}", [128, 1], F32) for i in range(8)]
        xt = [bd.sb(f"xt{j}", [128, 512], F32) for j in range(2)]
        otile = [bd.sb(f"ot{j}", [128, 512], F32) for j in range(4)]
        zrow = bd.sb("zrow", [1, 512], F32)
        rb_sb = bd.sb("rb_sb", [64, 512], F32)
        ones_r = bd.sb("ones_r", [1, 64], F32)
        ps_mm = [bd.psum(f"psmm{j}", [128, 512], F32) for j in range(2)]
        ps_b = bd.psum("psb", [64, 512], F32)
        rot = PsRot(bd, ps_mm, ["R0", "R1"])
        rv = lambda t: t[:, :].rearrange("p (m c2) -> p m c2", m=32)
        for c in range(8):
            bd.dma("sync", rv(kvp_sb[c]),
                   KVP[c, :, :].rearrange("(m p) c2 -> p m c2", p=128), "lkv")
        for i in range(8):
            bd.dma("gpsimd", wosb[i][:, :], WOUT[128 * i:128 * (i + 1), :], "lwo")
            bd.dma("sync", bout_c[i][:, :], BOUT[128 * i:128 * (i + 1), :], "lbo")
        t_on = bd.emit("gpsimd", lambda e: e.memset(ones_r[:, :], 1.0))
        bd.wait_tok("tensor", t_on)
        bd.wait("vector", "lkv")
        for m in range(32):
            for step in (1, 2, 4):
                for c in range(0, 8, 2 * step):
                    bd.emit("vector", lambda e, m=m, c=c, step=step: e.tensor_add(
                        rv(kvp_sb[c])[:, m, :], rv(kvp_sb[c])[:, m, :],
                        rv(kvp_sb[c + step])[:, m, :]))
            bd.emit("vector", lambda e, m=m: e.tensor_copy(
                kv_bf[m][:, 0:65], rv(kvp_sb[0])[:, m, :]), inc="kvr")
        for h01 in range(2):
            for fh in range(2):
                bd.dma("sync", qp_sb[2 * h01 + fh][:, :],
                       QP[256 * h01 + 128 * fh:256 * h01 + 128 * (fh + 1), :], f"lqp{h01}")
        bd.wait("tensor", "kvr")
        prev_zb = None
        for h in range(16):
            jq = h % 2
            if h >= 2:
                bd.wait("sync", "attmm", (h - 1) * NQ)
                for fh in range(2):
                    bd.dma("sync", qp_sb[2 * jq + fh][:, :],
                           QP[256 * h + 128 * fh:256 * h + 128 * (fh + 1), :], f"lqp{jq}")
            bd.wait("tensor", f"lqp{jq}")
            for q in range(NQ):
                w = min(512, T - q * 512)
                ps = rot.acquire()
                for fh in range(2):
                    bd.emit("tensor", lambda e, h=h, jq=jq, fh=fh, q=q, w=w, ps=ps: e.matmul(
                        ps[0:65, 0:w], kv_bf[2 * h + fh][:, 0:65],
                        qp_sb[2 * jq + fh][:, q * 512:q * 512 + w],
                        start=(fh == 0), stop=(fh == 1)),
                        inc=("attmm") if fh == 1 else None)
                bd.wait("vector", "attmm")
                t1 = bd.emit("vector", lambda e, w=w, ps=ps: e.tensor_scalar_add(
                    zrow[0:1, 0:w], ps[64:65, 0:w], 1e-8))
                t2 = bd.emit("vector", lambda e, w=w: e.reciprocal(
                    zrow[0:1, 0:w], zrow[0:1, 0:w]))
                bd.wait_tok("tensor", t2)
                if prev_zb is not None:
                    bd.wait("tensor", prev_zb[0], prev_zb[1])
                bd.emit("tensor", lambda e, w=w: e.matmul(
                    ps_b[:, 0:w], ones_r[:, :], zrow[0:1, 0:w], start=True, stop=True),
                    inc="zbm")
                bd.wait("vector", "zbm")
                g, hh = h // 2, h % 2
                tc = bd.emit("vector", lambda e, w=w: e.tensor_copy(
                    rb_sb[0:64, 0:w], ps_b[0:64, 0:w]))
                t3 = bd.emit("vector", lambda e, g=g, hh=hh, q=q, w=w, ps=ps: e.tensor_mul(
                    att[g][64 * hh:64 * (hh + 1), q * 512:q * 512 + w],
                    ps[0:64, 0:w], rb_sb[0:64, 0:w]), inc="attv")
                prev_zb = ("self_vector", t3[1])
                rot.release(t3)
        bd.wait("tensor", "attv")
        bd.wait("tensor", "lwo")
        bd.wait("vector", "lbo")
        ti = 0
        for i in range(8):
            for q in range(NQ):
                w = min(512, T - q * 512)
                ps = rot.acquire()
                for k in range(8):
                    bd.emit("tensor", lambda e, i=i, k=k, q=q, w=w, ps=ps: e.matmul(
                        ps[:, 0:w], wosb[k][:, 128 * i:128 * (i + 1)],
                        att[k][:, q * 512:q * 512 + w],
                        start=(k == 0), stop=(k == 7)),
                        inc=("fom") if k == 7 else None)
                jx = ti % 2
                bd.dma("sync", xt[jx][:, 0:w], X[128 * i:128 * (i + 1), q * 512:q * 512 + w],
                       f"xl{jx}")
                bd.wait("vector", "fom")
                bd.wait("vector", f"xl{jx}")
                jo = ti % 4
                if ti >= 4:
                    bd.wait("vector", f"ost{jo}", 16 * (ti // 4))
                t9 = bd.emit("vector", lambda e, i=i, jx=jx, jo=jo, w=w, ps=ps:
                             e.scalar_tensor_tensor(
                                 otile[jo][:, 0:w], ps[:, 0:w], bout_c[i][:, 0:1],
                                 xt[jx][:, 0:w], ALU.add, ALU.add))
                rot.release(t9)
                bd.wait_tok("sync", t9)
                bd.dma("sync", OUT[128 * i:128 * (i + 1), q * 512:q * 512 + w],
                       otile[jo][:, 0:w], f"ost{jo}")
                ti += 1
        for j in range(4):
            bd.wait("sync", f"ost{j}")
        bd.finalize()
    return nc


def build_plnb(T):
    nc = bass.Bass("TRN2", target_bir_lowering=False)
    X = nc.dram_tensor("x", [D, T], F32, kind="ExternalInput").ap()
    Y = nc.dram_tensor("y", [D, T], BF16, kind="ExternalOutput").ap()
    es = contextlib.ExitStack()
    with es:
        bd = Bld(nc, es)
        x_sb = [bd.sb(f"x{i}", [128, T], F32) for i in range(8)]
        y_sb = [bd.sb(f"y{i}", [128, T], BF16) for i in range(8)]
        ctx = make_ln_ctx(bd, T)
        for i in range(8):
            bd.dma("sync", x_sb[i][:, :], X[128 * i:128 * (i + 1), :], "ld")
        emit_ln(bd, ctx, x_sb, y_sb, "A", ready=("ld", bd.cnt("ld")))
        bd.wait("sync", "Ause")
        for i in range(8):
            bd.dma("sync", Y[128 * i:128 * (i + 1), :], y_sb[i][:, :], "st")
        bd.wait("sync", "st")
        bd.finalize()
    return nc


def build_pb1(S_):
    nc = bass.Bass("TRN2", target_bir_lowering=False)
    ap = lambda n, s, d=F32, k="ExternalInput": nc.dram_tensor(n, s, d, kind=k).ap()
    XN = ap("xn", [D, S_], BF16)
    WQ4 = ap("wq4", [D, 256]); WK4 = ap("wk4", [D, 256]); WV4 = ap("wv4", [D, 256])
    BQ4 = ap("bq4", [1, 256]); BK4 = ap("bk4", [1, 256]); BV4 = ap("bv4", [1, 256])
    MB = ap("mb", [4, 128, 512])
    IDEN = ap("iden", [128, 128])
    HO = ap("ho", [256, S_], BF16, k="ExternalOutput")
    NC_ = S_ // 512
    NT = S_ // 128
    G = 2
    es = contextlib.ExitStack()
    with es:
        bd = Bld(nc, es)
        xs = [[bd.sb(f"xs{j}_{i}", [128, 512], BF16) for i in range(8)] for j in range(2)]
        q_sb = [bd.sb(f"q{m}", [128, S_], BF16) for m in range(2)]
        k_sb = [bd.sb(f"k{m}", [128, S_], BF16) for m in range(2)]
        v_sb = [bd.sb(f"v{t}", [128, 256], BF16) for t in range(NT)]
        e_sb = [bd.sb(f"e{j}", [128, S_], BF16) for j in range(2)]
        oacc = bd.sb("oacc", [64, S_], F32)
        obf = [bd.sb(f"obf{j}", [64, 2048], BF16) for j in range(2)]
        wq_sb = [bd.sb(f"wqs{i}", [128, 256], BF16) for i in range(8)]
        wk_sb = [bd.sb(f"wks{i}", [128, 256], BF16) for i in range(8)]
        wv_sb = [bd.sb(f"wvs{i}", [128, 256], BF16) for i in range(8)]
        bq_sb = bd.sb("bq_sb", [1, 256], BF16)
        bk_sb = bd.sb("bk_sb", [1, 256], BF16)
        bv_sb = bd.sb("bv_sb", [1, 256], BF16)
        mb_sb = [bd.sb(f"mb{v}", [128, 512], BF16) for v in range(4)]
        iden = bd.sb("iden_sb", [128, 128], BF16)
        ones5 = bd.sb("ones5", [1, 512], BF16)
        csch = bd.sb("csch", [128, 16], F32)
        cscol = bd.sb("cscol", [128, 1], F32)
        rccol = bd.sb("rccol", [128, 1], F32)
        vt_sb = [bd.sb(f"vt{j}", [128, 64], BF16) for j in range(2)]
        ps_big = [bd.psum(f"psb{j}", [128, 512], F32) for j in range(4)]
        ps_av = [bd.psum(f"psav{j}", [64, 512], F32) for j in range(2)]
        rot = PsRot(bd, ps_big, ["B0", "B1", "B2", "B3"])
        rotA = PsRot(bd, ps_av, ["A0", "A1"])
        for i in range(8):
            bd.dma("gpsimd", wq_sb[i][:, :], WQ4[128 * i:128 * (i + 1), :], "lw")
            bd.dma("gpsimd", wk_sb[i][:, :], WK4[128 * i:128 * (i + 1), :], "lw")
            bd.dma("gpsimd", wv_sb[i][:, :], WV4[128 * i:128 * (i + 1), :], "lw")
        bd.dma("gpsimd", bq_sb[:, :], BQ4[:, :], "lw")
        bd.dma("gpsimd", bk_sb[:, :], BK4[:, :], "lw")
        bd.dma("gpsimd", bv_sb[:, :], BV4[:, :], "lw")
        for v in range(4):
            bd.dma("gpsimd", mb_sb[v][:, :], MB[v, :, :], "lw")
        bd.dma("gpsimd", iden[:, :], IDEN[:, :], "lw")
        t_on = bd.emit("gpsimd", lambda e: e.memset(ones5[:, :], 1.0))
        bd.wait("tensor", "lw")
        bd.wait_tok("tensor", t_on)
        for qc in range(NC_):
            j = qc % 2
            if qc >= 2:
                bd.wait("sync", "xdone", qc - 1)
            for i in range(8):
                bd.dma("sync", xs[j][i][:, :],
                       XN[128 * i:128 * (i + 1), 512 * qc:512 * (qc + 1)], f"lx{j}")
            bd.wait("tensor", f"lx{j}")
            for (wsb, bsb, dst, scale) in ((wq_sb, bq_sb, q_sb, 0.125),
                                           (wk_sb, bk_sb, k_sb, 1.0)):
                for m in range(2):
                    ps = rot.acquire()
                    for i in range(8):
                        bd.emit("tensor", lambda e, i=i, m=m, j=j, ps=ps, wsb=wsb: e.matmul(
                            ps[:, 0:512], wsb[i][:, 128 * m:128 * (m + 1)],
                            xs[j][i][:, 0:512], start=(i == 0), stop=False))
                    bd.emit("tensor", lambda e, m=m, ps=ps, bsb=bsb: e.matmul(
                        ps[:, 0:512], bsb[0:1, 128 * m:128 * (m + 1)], ones5[0:1, 0:512],
                        start=False, stop=True), inc="pmm")
                    bd.wait("scalar", "pmm")
                    tk = bd.emit("scalar", lambda e, m=m, qc=qc, ps=ps, dst=dst, scale=scale:
                                 e.activation(dst[m][:, 512 * qc:512 * (qc + 1)],
                                              ps[:, 0:512], AF.Copy, scale=scale))
                    rot.release(tk)
            for tt in range(4):
                ps = rot.acquire()
                for i in range(8):
                    bd.emit("tensor", lambda e, i=i, tt=tt, j=j, ps=ps: e.matmul(
                        ps[:, 0:256], xs[j][i][:, 128 * tt:128 * (tt + 1)],
                        wv_sb[i][:, 0:256], start=(i == 0), stop=False))
                bd.emit("tensor", lambda e, ps=ps: e.matmul(
                    ps[:, 0:256], ones5[0:1, 0:128], bv_sb[0:1, 0:256],
                    start=False, stop=True),
                    inc=("pmm" if tt < 3 else "xdone"))
                if tt == 3:
                    bd.wait("scalar", "xdone")
                else:
                    bd.wait("scalar", "pmm")
                tk = bd.emit("scalar", lambda e, tt=tt, qc=qc, ps=ps: e.activation(
                    v_sb[4 * qc + tt][:, 0:256], ps[:, 0:256], AF.Copy))
                rot.release(tk)
        hst_cnt = 0
        for h in range(4):
            mi, base = h // 2, 64 * (h % 2)
            laste = {0: 0, 1: 0}
            lastvt = {0: 0, 1: 0}
            for p in range(NT // G):
                qs0 = (p * G * 128) // 512
                nqc = NC_ - qs0
                for gi in range(G):
                    kt = p * G + gi
                    ebuf = e_sb[gi]
                    if laste[gi]:
                        bd.wait("scalar", "avm", laste[gi])
                    for qc in range(qs0, NC_):
                        ps = rot.acquire()
                        diag = (qc == qs0)
                        bd.emit("tensor", lambda e, mi=mi, base=base, kt=kt, qc=qc, ps=ps,
                                diag=diag: e.matmul(
                                    ps[:, 0:512],
                                    k_sb[mi][base:base + 64, 128 * kt:128 * (kt + 1)],
                                    q_sb[mi][base:base + 64, 512 * qc:512 * (qc + 1)],
                                    start=True, stop=not diag),
                                inc=None if diag else "smm")
                        if diag:
                            var = (kt * 128 - qs0 * 512) // 128
                            bd.emit("tensor", lambda e, ps=ps, var=var: e.matmul(
                                ps[:, 0:512], iden[:, :], mb_sb[var][:, 0:512],
                                start=False, stop=True), inc="smm")
                        bd.wait("scalar", "smm")
                        tk = bd.emit("scalar", lambda e, gi=gi, qc=qc, ps=ps, ebuf=ebuf:
                                     e.activation(ebuf[:, 512 * qc:512 * (qc + 1)],
                                                  ps[:, 0:512], AF.Exp,
                                                  accum_out=csch[:, qc:qc + 1]),
                                     inc="expk")
                        rot.release(tk)
                    bd.wait("vector", "expk")
                    if lastvt[gi]:
                        bd.wait("vector", "avm", lastvt[gi])
                    t1 = bd.emit("vector", lambda e, qs0=qs0: e.tensor_reduce(
                        cscol[:, 0:1], csch[:, qs0:NC_], AX.X, ALU.add))
                    t2 = bd.emit("vector", lambda e: e.reciprocal(rccol[:, 0:1], cscol[:, 0:1]))
                    t3 = bd.emit("vector", lambda e, gi=gi, kt=kt, h=h: e.tensor_scalar_mul(
                        vt_sb[gi][:, 0:64], v_sb[kt][:, 64 * h:64 * (h + 1)],
                        rccol[:, 0:1]), inc="vtd")
                    bd.wait_tok("scalar", t1)
                bd.wait("tensor", "vtd")
                bd.wait("tensor", "expk")
                for qc in range(qs0, NC_):
                    psA = rotA.acquire()
                    for gi in range(G):
                        bd.emit("tensor", lambda e, gi=gi, qc=qc, psA=psA: e.matmul(
                            psA[0:64, 0:512], vt_sb[gi][:, 0:64],
                            e_sb[gi][:, 512 * qc:512 * (qc + 1)],
                            start=(gi == 0), stop=(gi == G - 1)),
                            inc=("avm") if gi == G - 1 else None)
                    bd.wait("vector", "avm")
                    if p == 0:
                        tv = bd.emit("vector", lambda e, qc=qc, psA=psA: e.tensor_copy(
                            oacc[:, 512 * qc:512 * (qc + 1)], psA[0:64, 0:512]))
                    else:
                        tv = bd.emit("vector", lambda e, qc=qc, psA=psA: e.tensor_add(
                            oacc[:, 512 * qc:512 * (qc + 1)],
                            oacc[:, 512 * qc:512 * (qc + 1)], psA[0:64, 0:512]))
                    rotA.release(tv)
                laste[0] = laste[1] = bd.cnt("avm")
                lastvt[0] = lastvt[1] = bd.cnt("avm")
            for pc in range(S_ // 2048):
                jb = pc % 2
                hoidx = h * (S_ // 2048) + pc
                if hoidx >= 2:
                    bd.wait("vector", f"hst{jb}", 16 * (hoidx // 2))
                th = bd.emit("vector", lambda e, jb=jb, pc=pc: e.tensor_copy(
                    obf[jb][:, 0:2048], oacc[:, 2048 * pc:2048 * (pc + 1)]), inc="hod")
                bd.wait_tok("sync", th)
                bd.dma("sync", HO[64 * h:64 * (h + 1), 2048 * pc:2048 * (pc + 1)],
                       obf[jb][:, 0:2048], f"hst{jb}")
        bd.wait("sync", "hst0")
        bd.wait("sync", "hst1")
        bd.finalize()
    return nc


def build_pb2(T):
    nc = bass.Bass("TRN2", target_bir_lowering=False)
    ap = lambda n, s, d=F32, k="ExternalInput": nc.dram_tensor(n, s, d, kind=k).ap()
    X = ap("x", [D, T])
    HOC = ap("hoc", [D, T], BF16)
    WC = ap("wc", [D, D])
    BC = ap("bc", [D, 1])
    OUT = ap("out", [D, T], F32, k="ExternalOutput")
    NQ = cdiv(T, 512)
    es = contextlib.ExitStack()
    with es:
        bd = Bld(nc, es)
        ho_sb = [bd.sb(f"ho{i}", [128, T], BF16) for i in range(8)]
        wc_sb = [bd.sb(f"wc{i}", [128, D], BF16) for i in range(8)]
        bc_c = [bd.sb(f"bc{i}", [128, 1], F32) for i in range(8)]
        xt = [bd.sb(f"xt{j}", [128, 512], F32) for j in range(2)]
        otile = [bd.sb(f"ot{j}", [128, 512], F32) for j in range(4)]
        ps_mm = [bd.psum(f"psmm{j}", [128, 512], F32) for j in range(2)]
        rot = PsRot(bd, ps_mm, ["R0", "R1"])
        for i in range(8):
            bd.dma("sync", ho_sb[i][:, :], HOC[128 * i:128 * (i + 1), :], "lho")
            bd.dma("gpsimd", wc_sb[i][:, :], WC[128 * i:128 * (i + 1), :], "lwc")
            bd.dma("sync", bc_c[i][:, :], BC[128 * i:128 * (i + 1), :], "lbc")
        bd.wait("tensor", "lho")
        bd.wait("tensor", "lwc")
        bd.wait("vector", "lbc")
        ti = 0
        for i in range(8):
            for q in range(NQ):
                w = min(512, T - q * 512)
                ps = rot.acquire()
                for k in range(8):
                    bd.emit("tensor", lambda e, i=i, k=k, q=q, w=w, ps=ps: e.matmul(
                        ps[:, 0:w], wc_sb[k][:, 128 * i:128 * (i + 1)],
                        ho_sb[k][:, q * 512:q * 512 + w],
                        start=(k == 0), stop=(k == 7)),
                        inc=("fom") if k == 7 else None)
                jx = ti % 2
                bd.dma("sync", xt[jx][:, 0:w], X[128 * i:128 * (i + 1), q * 512:q * 512 + w],
                       f"xl{jx}")
                bd.wait("vector", "fom")
                bd.wait("vector", f"xl{jx}")
                jo = ti % 4
                if ti >= 4:
                    bd.wait("vector", f"ost{jo}", 16 * (ti // 4))
                t9 = bd.emit("vector", lambda e, i=i, jx=jx, jo=jo, w=w, ps=ps:
                             e.scalar_tensor_tensor(
                                 otile[jo][:, 0:w], ps[:, 0:w], bc_c[i][:, 0:1],
                                 xt[jx][:, 0:w], ALU.add, ALU.add))
                rot.release(t9)
                bd.wait_tok("sync", t9)
                bd.dma("sync", OUT[128 * i:128 * (i + 1), q * 512:q * 512 + w],
                       otile[jo][:, 0:w], f"ost{jo}")
                ti += 1
        for j in range(4):
            bd.wait("sync", f"ost{j}")
        bd.finalize()
    return nc


def build_pa_att(TB=1024):
    HB = 513
    YW = TB + HB
    QR = TB + 128
    KR = TB + 384
    NKT = KR // 128
    NQT = QR // 128
    nc = bass.Bass("TRN2", target_bir_lowering=False)
    ap = lambda n, s, d=F32, k="ExternalInput": nc.dram_tensor(n, s, d, kind=k).ap()
    Y = ap("y", [D, YW])
    WQK = ap("wqk", [D, 2048])
    WV = ap("wv", [D, D])
    WO = ap("wo", [D, D])
    QSL = ap("qsl", [128, 1]); QSH = ap("qsh", [128, 1])
    KSL = ap("ksl", [128, 1]); KSH = ap("ksh", [128, 1])
    CSQ = ap("csq", [128, YW]); SSQ = ap("ssq", [128, YW])
    CSK = ap("csk", [128, YW]); SSK = ap("ssk", [128, YW])
    MBM = ap("mb", [NKT, 128, 384])
    IDEN = ap("iden", [128, 128])
    IND32 = ap("ind32", [128, 4])
    IND4 = ap("ind4", [4, 128])
    YP = ap("yp", [D, TB + 1], F32, k="ExternalOutput")
    es = contextlib.ExitStack()
    with es:
        bd = Bld(nc, es)
        ybf = [bd.sb(f"ybf{i}", [128, YW], BF16) for i in range(8)]
        yres = [bd.sb(f"yres{j}", [128, 128], F32) for j in range(2)]
        q_sb = [bd.sb(f"q{i}", [128, QR], BF16) for i in range(8)]
        k_sb = [bd.sb(f"k{i}", [128, KR], BF16) for i in range(8)]
        vext = [bd.sb(f"vx{t}", [128, 16 * 65], BF16) for t in range(KR // 128)]
        e_sb = [bd.sb(f"e{h}", [128, 3 * 384], BF16) for h in range(16)]
        wslot = [bd.sb(f"ws{i}", [128, D], BF16) for i in range(8)]
        tabs = {n: bd.sb(f"tab_{n}", [128, YW], F32) for n in ("csq", "ssq", "csk", "ssk")}
        scol = {n: bd.sb(f"sc_{n}", [128, 1], F32) for n in ("qsl", "qsh", "ksl", "ksh",
                                                             "qslm", "qshm", "kslm", "kshm")}
        mb_sb = [bd.sb(f"mb{t}", [128, 384], BF16) for t in range(NKT)]
        iden = bd.sb("iden_sb", [128, 128], BF16)
        ones5 = bd.sb("ones5", [1, 512], BF16)
        ones_f = bd.sb("ones_f64", [1, 64], F32)
        rsc = [bd.sb(f"rsc{j}", [128, 512], F32) for j in range(4)]
        att_bf = [bd.sb(f"att{j}", [128, 128], BF16) for j in range(16)]
        rrow = bd.sb("rrow", [1, 128], F32)
        rb64 = bd.sb("rb64", [64, 128], F32)
        ind32 = bd.sb("ind32_sb", [128, 4], BF16)
        ind4 = bd.sb("ind4_sb", [4, 128], F32)
        sqb = [bd.sb(f"sqb{j}", [128, 512], BF16) for j in range(2)]
        nrm_g = [bd.sb("nrm0", [4, TB + 384], F32)]
        nrow = bd.sb("nrow", [4, TB + 384], F32)
        ot = [bd.sb(f"ot{j}", [128, 512], F32) for j in range(2)]
        ps_mm = [bd.psum(f"psmm{j}", [128, 512], F32) for j in range(2)]
        ps_s = [bd.psum(f"pss{j}", [128, 384], F32) for j in range(2)]
        ps_av = [bd.psum(f"psav{j}", [65, 128], F32) for j in range(3)]
        ps_rb = bd.psum("psrb", [64, 128], F32)
        rot = PsRot(bd, ps_mm, ["B0", "B1"])
        rotS = PsRot(bd, ps_s, ["S0", "S1"])
        rotA = PsRot(bd, ps_av, ["A0", "A1", "A2"])
        for i in range(8):
            bd.dma("gpsimd", ybf[i][:, :], Y[128 * i:128 * (i + 1), :], "ly")
        for n, t in tabs.items():
            bd.dma("sync", t[:, :], {"csq": CSQ, "ssq": SSQ, "csk": CSK, "ssk": SSK}[n][:, :],
                   "ltab")
        for n in ("qsl", "qsh", "ksl", "ksh"):
            bd.dma("sync", scol[n][:, :], {"qsl": QSL, "qsh": QSH, "ksl": KSL,
                                           "ksh": KSH}[n][:, :], "ltab")
        for t in range(NKT):
            bd.dma("gpsimd", mb_sb[t][:, :], MBM[t, :, :], "lmb")
        bd.dma("gpsimd", iden[:, :], IDEN[:, :], "lmb")
        bd.dma("gpsimd", ind32[:, :], IND32[:, :], "lmb")
        bd.dma("sync", ind4[:, :], IND4[:, :], "ltab")
        t_on = bd.emit("gpsimd", lambda e: e.memset(ones5[:, :], 1.0))
        t_on = bd.emit("gpsimd", lambda e: e.memset(ones_f[:, :], 1.0))
        bd.wait("vector", "ltab")
        for n in ("qsl", "qsh", "ksl", "ksh"):
            bd.emit("vector", lambda e, n=n: e.tensor_scalar_mul(
                scol[n + "m"][:, 0:1], scol[n][:, 0:1], -1.0), inc="negs")
        bd.wait("tensor", "ly")
        def rhs_sh(i, c0, w):
            if i < 4:
                return ybf[i][:, c0:c0 + w]
            return ybf[i][:, c0 - 1:c0 + w - 1]
        for part, dst, ncols, ybase, wofs in (("q", q_sb, QR, HB - 128, 0),
                                              ("k", k_sb, KR, HB - 384, 1024)):
            if part == "k":
                bd.wait("gpsimd", "qmm")
            for i in range(8):
                bd.dma("gpsimd", wslot[i][:, :], WQK[128 * i:128 * (i + 1),
                                                     wofs:wofs + 1024], part + "wl")
            bd.wait("tensor", part + "wl")
            for m in range(8):
                for c in range(0, ncols, 512):
                    w = min(512, ncols - c)
                    ps = rot.acquire()
                    for i in range(8):
                        bd.emit("tensor", lambda e, i=i, m=m, c=c, w=w, ps=ps, ybase=ybase:
                                e.matmul(ps[:, 0:w], wslot[i][:, 128 * m:128 * (m + 1)],
                                         rhs_sh(i, ybase + c, w),
                                         start=(i == 0), stop=(i == 7)),
                                inc=(part + "mm") if i == 7 else None)
                    bd.wait("vector", part + "mm")
                    tv = bd.emit("vector", lambda e, m=m, c=c, w=w, ps=ps, dst=dst:
                                 e.tensor_copy(dst[m][:, c:c + w], ps[:, 0:w]),
                                 inc=part + "ev")
                    rot.release(tv)
        bd.wait("gpsimd", "kmm")
        for i in range(8):
            bd.dma("gpsimd", wslot[i][:, :], WV[128 * i:128 * (i + 1), :], "vwl")
        for t in range(KR // 128):
            bd.emit("gpsimd", lambda e, t=t: e.memset(
                vext[t][:, :].rearrange("p (h c) -> p h c", h=16)[:, :, 64:65], 1.0),
                inc="vone")
        bd.wait("tensor", "vwl")
        bd.wait("tensor", "vone")
        for t in range(KR // 128):
            for half in range(2):
                ps = rot.acquire()
                for i in range(8):
                    bd.emit("tensor", lambda e, i=i, t=t, half=half, ps=ps: e.matmul(
                        ps[:, 0:512], rhs_sh(i, HB - 384 + 128 * t, 128),
                        wslot[i][:, 512 * half:512 * (half + 1)],
                        start=(i == 0), stop=(i == 7)),
                        inc=("vmm") if i == 7 else None)
                bd.wait("vector", "vmm")
                hbase = half * 8
                tv = bd.emit("vector", lambda e, t=t, hbase=hbase, ps=ps: e.tensor_copy(
                    vext[t][:, :].rearrange("p (h c) -> p h c", h=16)[:, hbase:hbase + 8, 0:64],
                    ps[:, 0:512].rearrange("p (h c) -> p h c", h=8)),
                    inc="vev")
                rot.release(tv)
        bd.wait("tensor", "lmb")
        bd.wait("vector", "negs")
        nrm = nrm_g[0]
        for part, dst, ncols, tbase, cs, ss, sl, sh, slm, shm in (
                ("q", q_sb, QR, HB - 128, "csq", "ssq", "qsl", "qsh", "qslm", "qshm"),
                ("k", k_sb, KR, HB - 384, "csk", "ssk", "ksl", "ksh", "kslm", "kshm")):
            bd.wait("scalar", part + "ev")
            bd.wait("vector", part + "ev")
            for g in range(4):
                lo, hi = dst[2 * g], dst[2 * g + 1]
                for c in range(0, ncols, 512):
                    w = min(512, ncols - c)
                    jj = (c // 512) % 2
                    if bd.cnt(part + "nm"):
                        bd.wait("scalar", part + "nm")
                    t1 = bd.emit("scalar", lambda e, c=c, w=w, jj=jj, lo=lo:
                                 e.activation(sqb[jj][:, 0:w], lo[:, c:c + w], AF.Square))
                    t2 = bd.emit("scalar", lambda e, c=c, w=w, jj=jj, hi=hi:
                                 e.activation(sqb[1 - jj][:, 0:w], hi[:, c:c + w],
                                              AF.Square), inc=part + f"sq{g}")
                    bd.wait("tensor", part + f"sq{g}")
                    ps = rot.acquire()
                    bd.emit("tensor", lambda e, w=w, jj=jj, ps=ps: e.matmul(
                        ps[0:4, 0:w], ind32[:, 0:4], sqb[jj][:, 0:w],
                        start=True, stop=False))
                    bd.emit("tensor", lambda e, w=w, jj=jj, ps=ps: e.matmul(
                        ps[0:4, 0:w], ind32[:, 0:4], sqb[1 - jj][:, 0:w],
                        start=False, stop=True), inc=part + "nm")
                    bd.wait("vector", part + "nm")
                    tv = bd.emit("vector", lambda e, c=c, w=w, ps=ps:
                                 e.tensor_copy(nrm[0:4, c:c + w], ps[0:4, 0:w]))
                    rot.release(tv)
                bd.emit("vector", lambda e, ncols=ncols: e.tensor_scalar_max(
                    nrm[0:4, 0:ncols], nrm[0:4, 0:ncols], 1e-24))
                bd.wait_tok("scalar", bd.tok("vector"))
                s1 = bd.emit("scalar", lambda e, ncols=ncols: e.activation(
                    nrow[0:4, 0:ncols], nrm[0:4, 0:ncols], AF.Ln))
                s2 = bd.emit("scalar", lambda e, ncols=ncols: e.activation(
                    nrm[0:4, 0:ncols], nrow[0:4, 0:ncols], AF.Exp, scale=0.5))
                bd.wait_tok("vector", s2)
                bd.emit("vector", lambda e, ncols=ncols: e.reciprocal(
                    nrm[0:4, 0:ncols], nrm[0:4, 0:ncols]), inc=part + f"rin{g}")
                for c in range(0, ncols, 512):
                    w = min(512, ncols - c)
                    tsl = slice(tbase + c, tbase + c + w)
                    bd.emit("vector", lambda e, lo=lo, c=c, w=w, tsl=tsl, cs=cs, sl=sl:
                            e.scalar_tensor_tensor(rsc[0][:, 0:w], lo[:, c:c + w],
                                                   scol[sl][:, 0:1], tabs[cs][:, tsl],
                                                   ALU.mult, ALU.mult))
                    bd.emit("vector", lambda e, hi=hi, c=c, w=w, tsl=tsl, ss=ss, shm=shm:
                            e.scalar_tensor_tensor(rsc[1][:, 0:w], hi[:, c:c + w],
                                                   scol[shm][:, 0:1], tabs[ss][:, tsl],
                                                   ALU.mult, ALU.mult))
                    bd.emit("vector", lambda e, hi=hi, c=c, w=w, tsl=tsl, cs=cs, sh=sh:
                            e.scalar_tensor_tensor(rsc[2][:, 0:w], hi[:, c:c + w],
                                                   scol[sh][:, 0:1], tabs[cs][:, tsl],
                                                   ALU.mult, ALU.mult))
                    bd.emit("vector", lambda e, lo=lo, c=c, w=w, tsl=tsl, ss=ss, sl=sl:
                            e.scalar_tensor_tensor(rsc[3][:, 0:w], lo[:, c:c + w],
                                                   scol[sl][:, 0:1], tabs[ss][:, tsl],
                                                   ALU.mult, ALU.mult))
                    bd.emit("vector", lambda e, lo=lo, c=c, w=w: e.tensor_add(
                        lo[:, c:c + w], rsc[0][:, 0:w], rsc[1][:, 0:w]))
                    bd.emit("vector", lambda e, hi=hi, c=c, w=w: e.tensor_add(
                        hi[:, c:c + w], rsc[2][:, 0:w], rsc[3][:, 0:w]))
                    ps = rot.acquire()
                    bd.wait("tensor", part + f"rin{g}")
                    bd.wait_tok("tensor", bd.tok("vector"))
                    bd.emit("tensor", lambda e, g=g, c=c, w=w, ps=ps: e.matmul(
                        ps[:, 0:w], ind4[0:4, :], nrm[0:4, c:c + w],
                        start=True, stop=True), inc=part + "rbc")
                    bd.wait("vector", part + "rbc")
                    bd.emit("vector", lambda e, lo=lo, c=c, w=w, ps=ps: e.tensor_mul(
                        lo[:, c:c + w], lo[:, c:c + w], ps[:, 0:w]))
                    tvx = bd.emit("vector", lambda e, hi=hi, c=c, w=w, ps=ps: e.tensor_mul(
                        hi[:, c:c + w], hi[:, c:c + w], ps[:, 0:w]), inc=part + "rot")
                    rot.release(tvx)
        bd.wait("tensor", "qrot")
        bd.wait("tensor", "krot")
        bd.wait("tensor", "vev")
        bd.wait("tensor", "lmb")
        bd.wait_tok("tensor", t_on)
        QK_SCALE = 8.0
        lastE = {}
        def emit_E(kt):
            qq0 = 128 * kt - 256
            off = max(0, -qq0)
            lo_c = qq0 + off
            wdt = min(384 - off, QR - lo_c)
            if wdt <= 0:
                return
            for h in range(16):
                g, hp = h // 4, h % 4
                base = 32 * hp
                tp = (base, 0)
                ps = rotS.acquire()
                bd.emit("tensor", lambda e, g=g, base=base, kt=kt, lo_c=lo_c, wdt=wdt,
                        ps=ps, off=off, tp=tp: e.matmul(
                            ps[:, off:off + wdt],
                            k_sb[2 * g][base:base + 32, 128 * kt:128 * (kt + 1)],
                            q_sb[2 * g][base:base + 32, lo_c:lo_c + wdt],
                            start=True, stop=False, tile_position=tp))
                bd.emit("tensor", lambda e, g=g, base=base, kt=kt, lo_c=lo_c, wdt=wdt,
                        ps=ps, off=off, tp=tp: e.matmul(
                            ps[:, off:off + wdt],
                            k_sb[2 * g + 1][base:base + 32, 128 * kt:128 * (kt + 1)],
                            q_sb[2 * g + 1][base:base + 32, lo_c:lo_c + wdt],
                            start=False, stop=False, tile_position=tp))
                bd.emit("tensor", lambda e, kt=kt, ps=ps, off=off, wdt=wdt: e.matmul(
                    ps[:, off:off + wdt], iden[:, :], mb_sb[kt][:, off:off + wdt],
                    start=False, stop=True), inc="smm")
                bd.wait("scalar", "smm")
                if (h, kt % 3) in lastE:
                    bd.wait("scalar", "avm", lastE[(h, kt % 3)])
                tk = bd.emit("scalar", lambda e, h=h, kt=kt, ps=ps, off=off, wdt=wdt:
                             e.activation(e_sb[h][:, 384 * (kt % 3) + off:
                                                  384 * (kt % 3) + off + wdt],
                                          ps[:, off:off + wdt], AF.Exp, scale=QK_SCALE),
                             inc="ek")
                rotS.release(tk)
        emit_E(0)
        emit_E(1)
        wo_loaded = False
        for qt in range(NQT):
            kt_new = qt + 2
            if kt_new < NKT:
                emit_E(kt_new)
            bd.wait("tensor", "ek")
            for h in range(16):
                psA = rotA.acquire()
                for s, kt in enumerate((qt, qt + 1, qt + 2)):
                    qq0 = 128 * kt - 256
                    col = 384 * (kt % 3) + (128 * qt - qq0)
                    bd.emit("tensor", lambda e, h=h, kt=kt, col=col, psA=psA, s=s: e.matmul(
                        psA[0:65, 0:128], vext[kt][:, 65 * h:65 * h + 65],
                        e_sb[h][:, col:col + 128],
                        start=(s == 0), stop=(s == 2)),
                        inc=("avm") if s == 2 else None)
                lastE[(h, qt % 3)] = bd.cnt("avm")
                bd.wait("vector", "avm")
                bd.emit("vector", lambda e, psA=psA: e.tensor_scalar_add(
                    rrow[0:1, 0:128], psA[64:65, 0:128], 1e-38))
                t1 = bd.emit("vector", lambda e: e.reciprocal(
                    rrow[0:1, 0:128], rrow[0:1, 0:128]))
                bd.wait_tok("tensor", t1)
                bd.guard_wait("tensor", "psrb")
                bd.emit("tensor", lambda e: e.matmul(
                    ps_rb[:, 0:128], ones_f[0:1, 0:64], rrow[0:1, 0:128],
                    start=True, stop=True), inc="rbm")
                bd.wait("vector", "rbm")
                g2, hh = h // 2, h % 2
                tc = bd.emit("vector", lambda e: e.tensor_copy(
                    rb64[0:64, 0:128], ps_rb[0:64, 0:128]))
                bd.guard_set("psrb", "self_vector", tc[1])
                tv = bd.emit("vector", lambda e, g2=g2, hh=hh, qt=qt, psA=psA: e.tensor_mul(
                    att_bf[8 * (qt % 2) + g2][64 * hh:64 * (hh + 1), 0:128],
                    psA[0:64, 0:128], rb64[0:64, 0:128]), inc="atv")
                rotA.release(tv)
            if not wo_loaded:
                bd.wait("gpsimd", "vmm")
                for i in range(8):
                    bd.dma("gpsimd", wslot[i][:, :], WO[128 * i:128 * (i + 1), :], "wol")
                bd.wait("tensor", "wol")
                wo_loaded = True
            oc0 = 128 * qt - 1
            bd.wait("tensor", "atv")
            for i in range(8):
                ps = rot.acquire()
                for kk in range(8):
                    bd.emit("tensor", lambda e, i=i, kk=kk, qt=qt, ps=ps: e.matmul(
                        ps[:, 0:128], wslot[kk][:, 128 * i:128 * (i + 1)],
                        att_bf[8 * (qt % 2) + kk][:, 0:128],
                        start=(kk == 0), stop=(kk == 7)),
                        inc=("wom") if kk == 7 else None)
                jo = (qt * 8 + i) % 2
                if qt == 0:
                    bd.dma("sync", yres[jo][:, 0:1], Y[128 * i:128 * (i + 1), 512:513],
                           f"yl{jo}", allow_slow_non_contiguous=True)
                else:
                    bd.dma("sync", yres[jo][:, 0:128],
                           Y[128 * i:128 * (i + 1), 385 + 128 * qt:385 + 128 * (qt + 1)],
                           f"yl{jo}")
                bd.wait("vector", "wom")
                bd.wait("vector", f"yl{jo}")
                ti2 = qt * 8 + i
                if ti2 >= 2:
                    bd.wait("vector", f"st{jo}", 16 * (ti2 // 2))
                if qt == 0:
                    tv = bd.emit("vector", lambda e, ps=ps, jo=jo: e.tensor_add(
                        ot[jo][:, 0:1], ps[:, 127:128], yres[jo][:, 0:1]))
                    bd.wait_tok("sync", tv)
                    bd.dma("sync", YP[128 * i:128 * (i + 1), 0:1], ot[jo][:, 0:1],
                           f"st{jo}", allow_slow_non_contiguous=True)
                else:
                    tv = bd.emit("vector", lambda e, ps=ps, jo=jo: e.tensor_add(
                        ot[jo][:, 0:128], ps[:, 0:128], yres[jo][:, 0:128]))
                    bd.wait_tok("sync", tv)
                    bd.dma("sync", YP[128 * i:128 * (i + 1), 128 * qt - 127:128 * (qt + 1) - 127],
                           ot[jo][:, 0:128], f"st{jo}")
                rot.release(tv)
        for j in range(2):
            bd.wait("sync", f"st{j}")
        bd.finalize()
    return nc


PAIRW = [128] * 21 + [42]


def build_pa_ffn1(T=2048):
    nc = bass.Bass("TRN2", target_bir_lowering=False)
    ap = lambda n, s, d=F32, k="ExternalInput": nc.dram_tensor(n, s, d, kind=k).ap()
    YPI = ap("yp", [D, T + 1])
    GC = ap("g", [D, 1]); BLN = ap("bln", [D, 1])
    W1P = ap("w1p", [D, 5460])
    B1P = ap("b1p", [128, 44])
    HO = ap("ho", [22 * 128, T], BF16, k="ExternalOutput")
    NQ = T // 512
    es = contextlib.ExitStack()
    with es:
        bd = Bld(nc, es)
        yp_sb = [bd.sb(f"yp{i}", [128, T + 1], F32) for i in range(8)]
        xn = [bd.sb(f"xn{i}", [128, T], BF16) for i in range(8)]
        w1h = [bd.sb(f"w1h{i}", [128, 11 * 256], BF16) for i in range(8)]
        b1sb = bd.sb("b1sb", [128, 44], F32)
        gcol = [bd.sb(f"gc{i}", [128, 1], F32) for i in range(8)]
        bcol = [bd.sb(f"bc{i}", [128, 1], F32) for i in range(8)]
        gsb = bd.sb("gsb", [128, 512], BF16)
        hot = [bd.sb(f"hot{j}", [128, 512], BF16) for j in range(4)]
        ps_mm = [bd.psum(f"psmm{j}", [128, 512], F32) for j in range(4)]
        ps_M2 = [bd.psum(f"psM2{j}", [128, 512], F32) for j in range(2)]
        ps_small = bd.psum("ps_small", [128, 512], F32)
        ctx = make_ln_ctx(bd, T, ps_R=ps_mm[0:2], ps_M=ps_M2, ps_small=ps_small)
        for i in range(8):
            bd.dma("sync", yp_sb[i][:, :], YPI[128 * i:128 * (i + 1), :], "ly")
            bd.dma("sync", gcol[i][:, :], GC[128 * i:128 * (i + 1), :], "lc")
            bd.dma("sync", bcol[i][:, :], BLN[128 * i:128 * (i + 1), :], "lc")
        bd.dma("sync", b1sb[:, :], B1P[:, :], "lc")
        bd.wait("vector", "lc")
        bd.wait("scalar", "lc")
        emit_ln(bd, ctx, yp_sb, xn, "L", g_col=gcol, b_col=bcol,
                ready=("ly", bd.cnt("ly")), x_off=[1, 1, 1, 1, 0, 0, 0, 0])
        bd.wait("tensor", "Luse")
        for half in range(2):
            jlo = 11 * half
            jhi = min(22, jlo + 11)
            if half == 1:
                bd.wait("gpsimd", "hmm")
            for i in range(8):
                bd.dma("gpsimd", w1h[i][:, 0:sum(2 * w for w in PAIRW[jlo:jhi])],
                       W1P[128 * i:128 * (i + 1),
                           256 * jlo:256 * jlo + sum(2 * w for w in PAIRW[jlo:jhi])],
                       f"w1l{half}")
            bd.wait("tensor", f"w1l{half}")
            for j in range(jlo, jhi):
                wj = PAIRW[j]
                co = 256 * (j - jlo)
                for q in range(NQ):
                    pa = ps_mm[2 * (q % 2)]
                    pg = ps_mm[2 * (q % 2) + 1]
                    ti = (j - jlo) * NQ + q + half * 100000
                    bd.guard_wait("tensor", f"ffA{q % 2}")
                    bd.guard_wait("tensor", f"ffG{q % 2}")
                    for i in range(8):
                        bd.emit("tensor", lambda e, i=i, co=co, wj=wj, q=q, pa=pa: e.matmul(
                            pa[:, 0:512][0:wj, :] if False else pa[0:wj, 0:512],
                            w1h[i][:, co:co + wj],
                            xn[i][:, 512 * q:512 * (q + 1)],
                            start=(i == 0), stop=(i == 7)))
                    for i in range(8):
                        bd.emit("tensor", lambda e, i=i, co=co, wj=wj, q=q, pg=pg: e.matmul(
                            pg[0:wj, 0:512], w1h[i][:, co + wj:co + 2 * wj],
                            xn[i][:, 512 * q:512 * (q + 1)],
                            start=(i == 0), stop=(i == 7)),
                            inc=("hmm") if i == 7 else None)
                    bd.wait("scalar", "hmm")
                    if bd.cnt("hev"):
                        bd.wait("scalar", "hev")
                    tg = bd.emit("scalar", lambda e, j=j, wj=wj, pg=pg: e.activation(
                        gsb[0:wj, 0:512], pg[0:wj, 0:512], AF.Gelu,
                        bias=b1sb[0:wj, 2 * j + 1:2 * j + 2]))
                    bd.guard_set(f"ffG{q % 2}", "self_scalar", tg[1])
                    bd.wait_tok("vector", tg)
                    idxh = j * NQ + q
                    jo = idxh % 4
                    if idxh >= 4:
                        bd.wait("vector", f"hst{jo}", 16 * (idxh // 4))
                    tv = bd.emit("vector", lambda e, j=j, wj=wj, q=q, pa=pa, jo=jo:
                                 e.scalar_tensor_tensor(
                                     hot[jo][0:wj, 0:512], pa[0:wj, 0:512],
                                     b1sb[0:wj, 2 * j:2 * j + 1], gsb[0:wj, 0:512],
                                     ALU.add, ALU.mult), inc="hev")
                    bd.guard_set(f"ffA{q % 2}", "self_vector", tv[1])
                    bd.wait_tok("sync", tv)
                    bd.dma("sync", HO[128 * j:128 * j + wj, 512 * q:512 * (q + 1)],
                           hot[jo][0:wj, 0:512], f"hst{jo}")
        for j in range(4):
            bd.wait("sync", f"hst{j}")
        bd.finalize()
    return nc


def build_pa_ffn2(T=2048):
    nc = bass.Bass("TRN2", target_bir_lowering=False)
    ap = lambda n, s, d=F32, k="ExternalInput": nc.dram_tensor(n, s, d, kind=k).ap()
    HOI = ap("ho", [22 * 128, T], BF16)
    W2P = ap("w2p", [22 * 128, D])
    B2 = ap("b2", [D, 1])
    YPI = ap("yp", [D, T + 1])
    YN = ap("yn", [D, T], F32, k="ExternalOutput")
    NQ = T // 512
    es = contextlib.ExitStack()
    with es:
        bd = Bld(nc, es)
        w2sb = [bd.sb(f"w2{j}", [128, D], BF16) for j in range(22)]
        hsl = [bd.sb(f"hs{j}", [128, 512], BF16) for j in range(2)]
        b2c = [bd.sb(f"b2c{i}", [128, 1], F32) for i in range(8)]
        yres = [bd.sb(f"yr{j}", [128, 512], F32) for j in range(2)]
        ot = [bd.sb(f"ot{j}", [128, 512], F32) for j in range(4)]
        ps = [bd.psum(f"ps{i}", [128, 512], F32) for i in range(8)]
        for j in range(22):
            bd.dma("gpsimd", w2sb[j][0:PAIRW[j], :], W2P[128 * j:128 * j + PAIRW[j], :],
                   "lw2")
        for i in range(8):
            bd.dma("sync", b2c[i][:, :], B2[128 * i:128 * (i + 1), :], "lb2")
        bd.wait("tensor", "lw2")
        bd.wait("vector", "lb2")
        for q in range(NQ):
            if q >= 1:
                bd.wait("tensor", "oev", 8 * q)
            for j in range(22):
                jh = j % 2
                if j >= 2:
                    bd.wait("sync", "h2mm", (q * 22 + j - 1) * 8)
                bd.dma("sync", hsl[jh][0:PAIRW[j], 0:512],
                       HOI[128 * j:128 * j + PAIRW[j], 512 * q:512 * (q + 1)], f"hl{jh}")
                bd.wait("tensor", f"hl{jh}")
                for i in range(8):
                    bd.emit("tensor", lambda e, i=i, j=j, jh=jh, ps=ps: e.matmul(
                        ps[i][:, 0:512], w2sb[j][0:PAIRW[j], 128 * i:128 * (i + 1)],
                        hsl[jh][0:PAIRW[j], 0:512],
                        start=(j == 0), stop=(j == 21)),
                        inc="h2mm")
            for i in range(8):
                jy = i % 2
                bd.dma("sync", yres[jy][:, 0:512],
                       YPI[128 * i:128 * (i + 1), 1 + 512 * q:1 + 512 * (q + 1)], f"yl{jy}")
                bd.wait("vector", "h2mm")
                bd.wait("vector", f"yl{jy}")
                jo = i % 4
                if q * 8 + i >= 4:
                    bd.wait("vector", f"ost{jo}", 16 * ((q * 8 + i) // 4))
                t1 = bd.emit("vector", lambda e, i=i, jy=jy, jo=jo: e.scalar_tensor_tensor(
                    ot[jo][:, 0:512], ps[i][:, 0:512], b2c[i][:, 0:1], yres[jy][:, 0:512],
                    ALU.add, ALU.add), inc="oev")
                bd.wait_tok("sync", t1)
                bd.dma("sync", YN[128 * i:128 * (i + 1), 512 * q:512 * (q + 1)],
                       ot[jo][:, 0:512], f"ost{jo}")
        for j in range(4):
            bd.wait("sync", f"ost{j}")
        bd.finalize()
    return nc


def pa_tables(s0, TB, n):
    YW = TB + 513
    t = np.arange(s0 - 513, s0 + TB, dtype=np.float64)
    j = np.arange(32, dtype=np.float64)
    inv_freq = 1.0 / 10000.0 ** (2.0 * j / 64.0)
    freqs = t[None, :] * inv_freq[:, None]
    base = (2.0 * j + 0.4 * 64.0) / (1.4 * 64.0)
    power = (t - n // 2) / (WIN // 2)
    scale = base[:, None] ** power[None, :]
    cos, sin = np.cos(freqs), np.sin(freqs)
    def rep(a):
        return np.tile(a.astype(np.float32), (4, 1))
    return (rep(cos * scale), rep(sin * scale),
            rep(cos / scale), rep(sin / scale))


def pa_masks(s0, TB):
    NKT = (TB + 384) // 128
    mb = np.zeros((NKT, 128, 384), np.float32)
    for kt in range(NKT):
        k0 = s0 - 384 + 128 * kt
        q0 = s0 + 128 * kt - 384
        kk = k0 + np.arange(128)[:, None]
        qq = q0 + np.arange(384)[None, :]
        valid = (qq >= kk) & (qq - kk <= WIN) & (kk >= 0)
        mb[kt][~valid] = -1e30
    return mb


def wqk_perm_cols():
    cols = []
    for g in range(4):
        for half in range(2):
            for h in range(4 * g, 4 * g + 4):
                for d in range(32):
                    cols.append(h * 64 + half * 32 + d)
    return np.array(cols)


def qs_patterns(qs):
    lo = np.tile(qs[0:32], 4)[:, None].astype(np.float32)
    hi = np.tile(qs[32:64], 4)[:, None].astype(np.float32)
    return lo, hi


def w1_perm_idx():
    acols, gcols, rows = [], [], []
    cols = []
    hrow = []
    for j in range(22):
        wj = PAIRW[j]
        a0 = sum(PAIRW[:j])
        cols.extend(range(a0, a0 + wj))
        cols.extend(range(2730 + a0, 2730 + a0 + wj))
        hrow.extend(range(a0, a0 + wj))
    return np.array(cols), np.array(hrow)


def build_pa_inputs_layer(Wqkv_l, qs_l, ks_l, Wo_l, g_l, b_l, W1_l, b1_l, W2_l, b2_l):
    pc = wqk_perm_cols()
    wqk = np.concatenate([Wqkv_l[:, 0:1024][:, pc], Wqkv_l[:, 1024:2048][:, pc]],
                         axis=1).astype(np.float32)
    wv = np.ascontiguousarray(Wqkv_l[:, 2048:3072])
    qsl, qsh = qs_patterns(qs_l)
    ksl, ksh = qs_patterns(ks_l)
    c1, hrow = w1_perm_idx()
    w1p = np.ascontiguousarray(W1_l[:, np.concatenate(
        [np.arange(sum(PAIRW[:j]), sum(PAIRW[:j]) + PAIRW[j]).tolist() +
         (2730 + np.arange(sum(PAIRW[:j]), sum(PAIRW[:j]) + PAIRW[j])).tolist()
         for j in range(22)]).astype(int)])
    b1p = np.zeros((128, 44), np.float32)
    for j in range(22):
        wj = PAIRW[j]
        a0 = sum(PAIRW[:j])
        b1p[0:wj, 2 * j] = b1_l[a0:a0 + wj]
        b1p[0:wj, 2 * j + 1] = b1_l[2730 + a0:2730 + a0 + wj]
    w2p = np.zeros((22 * 128, 1024), np.float32)
    for j in range(22):
        wj = PAIRW[j]
        a0 = sum(PAIRW[:j])
        w2p[128 * j:128 * j + wj, :] = W2_l[a0:a0 + wj, :]
    return dict(wqk=wqk, wv=wv, wo=np.ascontiguousarray(Wo_l),
                qsl=qsl, qsh=qsh, ksl=ksl, ksh=ksh,
                g=g_l.reshape(D, 1).astype(np.float32),
                bln=b_l.reshape(D, 1).astype(np.float32),
                w1p=w1p, b1p=b1p, w2p=w2p, b2=b2_l.reshape(D, 1).astype(np.float32))


CH = 2048


def _core(b, c):
    return b * NCH + c


def kernel(**inp):
    x = np.asarray(inp["L0_attn_source"], dtype=np.float32)
    X = [np.ascontiguousarray(x[b].T) for b in range(B)]
    IDEN = np.eye(128, dtype=np.float32)
    IND32 = np.zeros((128, 4), np.float32)
    for hp in range(4):
        IND32[32 * hp:32 * (hp + 1), hp] = 1.0
    IND4 = np.ascontiguousarray(IND32.T)
    REP = np.repeat(np.eye(8, dtype=np.float32), 64, axis=1)

    def chunks(A):
        return [np.ascontiguousarray(A[b][:, c * CH:(c + 1) * CH])
                for b in range(B) for c in range(NCH)]

    def l0_ffn_phase():
        nc_pln = get_prog("pln", build_pln, CH)
        xs = chunks(X)
        res = run8(nc_pln, [{"x": xc} for xc in xs])
        Y = [np.zeros((D, S), np.float32) for _ in range(B)]
        for b in range(B):
            for c in range(NCH):
                Y[b][:, c * CH:(c + 1) * CH] = res[_core(b, c)]["y"]
        lays = [build_pa_inputs_layer(
            inp["L0_ffn_Wqkv"][l], inp["L0_ffn_q_scale"][l], inp["L0_ffn_k_scale"][l],
            inp["L0_ffn_Wo"][l], inp["L0_ffn_ln_g"][l], inp["L0_ffn_ln_b"][l],
            inp["L0_ffn_W1"][l], inp["L0_ffn_b1"][l], inp["L0_ffn_W2"][l],
            inp["L0_ffn_b2"][l]) for l in range(6)]
        TB = 1024
        nc_att = get_prog("pa_att", build_pa_att, TB)
        nc_f1 = get_prog("pa_ffn1", build_pa_ffn1, CH)
        nc_f2 = get_prog("pa_ffn2", build_pa_ffn2, CH)
        tab_cache = {}
        for s0 in range(0, S, TB):
            tab_cache[s0] = (pa_tables(s0, TB, S), pa_masks(s0, TB))
        for l in range(6):
            lay = lays[l]
            YPf = [np.zeros((D, S + 1), np.float32) for _ in range(B)]
            for half in range(2):
                maps = []
                for b in range(B):
                    for c in range(NCH):
                        s0 = c * CH + half * TB
                        yw = np.zeros((D, TB + 513), np.float32)
                        lo = max(0, s0 - 513)
                        if s0 > 0:
                            yw[:, 513 - (s0 - lo):513] = Y[b][:, lo:s0]
                        yw[:, 513:] = Y[b][:, s0:s0 + TB]
                        (csq, ssq, csk, ssk), mb = tab_cache[s0]
                        maps.append(dict(y=yw, wqk=lay["wqk"], wv=lay["wv"],
                                         wo=lay["wo"], qsl=lay["qsl"], qsh=lay["qsh"],
                                         ksl=lay["ksl"], ksh=lay["ksh"], csq=csq,
                                         ssq=ssq, csk=csk, ssk=ssk, mb=mb, iden=IDEN,
                                         ind32=IND32, ind4=IND4))
                res = run8(nc_att, maps)
                for b in range(B):
                    for c in range(NCH):
                        s0 = c * CH + half * TB
                        YPf[b][:, s0:s0 + TB + 1] = res[_core(b, c)]["yp"]
            for b in range(B):
                YPf[b][:, 0] = 0.0
            maps = [dict(yp=np.ascontiguousarray(YPf[b][:, c * CH:c * CH + CH + 1]),
                         g=lay["g"], bln=lay["bln"], w1p=lay["w1p"], b1p=lay["b1p"])
                    for b in range(B) for c in range(NCH)]
            res = run8(nc_f1, maps)
            maps2 = [dict(ho=res[_core(b, c)]["ho"], w2p=lay["w2p"], b2=lay["b2"],
                          yp=np.ascontiguousarray(YPf[b][:, c * CH:c * CH + CH + 1]))
                     for b in range(B) for c in range(NCH)]
            res2 = run8(nc_f2, maps2)
            for b in range(B):
                for c in range(NCH):
                    Y[b][:, c * CH:(c + 1) * CH] = res2[_core(b, c)]["yn"]
        nc_pax = get_prog("pax", build_pax, CH, 0.5)
        maps = [dict(x=np.ascontiguousarray(X[b][:, c * CH:(c + 1) * CH]),
                     y=np.ascontiguousarray(Y[b][:, c * CH:(c + 1) * CH]))
                for b in range(B) for c in range(NCH)]
        res = run8(nc_pax, maps)
        for b in range(B):
            for c in range(NCH):
                X[b][:, c * CH:(c + 1) * CH] = res[_core(b, c)]["o"]

    def l0_attn_phase():
        nc_plnb = get_prog("plnb", build_plnb, CH)
        res = run8(nc_plnb, [{"x": xc} for xc in chunks(X)])
        XN = [np.zeros((D, S), bf16) for _ in range(B)]
        for b in range(B):
            for c in range(NCH):
                XN[b][:, c * CH:(c + 1) * CH] = res[_core(b, c)]["y"]
        MBv = np.zeros((4, 128, 512), np.float32)
        for v in range(4):
            kk = np.arange(128)[:, None]
            qq = np.arange(512)[None, :]
            MBv[v][qq < v * 128 + kk] = -1e30
        Wq, Wk, Wv = inp["L0_attn_Wq"], inp["L0_attn_Wk"], inp["L0_attn_Wv"]
        bq, bk, bv = inp["L0_attn_bq"], inp["L0_attn_bk"], inp["L0_attn_bv"]
        nc_pb1 = get_prog("pb1", build_pb1, S)
        maps = []
        for b in range(B):
            for c in range(NCH):
                hs = slice(4 * c, 4 * c + 4)
                maps.append(dict(
                    xn=XN[b],
                    wq4=np.ascontiguousarray(Wq[hs].transpose(1, 0, 2).reshape(D, 256)),
                    wk4=np.ascontiguousarray(Wk[hs].transpose(1, 0, 2).reshape(D, 256)),
                    wv4=np.ascontiguousarray(Wv[hs].transpose(1, 0, 2).reshape(D, 256)),
                    bq4=np.ascontiguousarray(bq[hs].reshape(1, 256)),
                    bk4=np.ascontiguousarray(bk[hs].reshape(1, 256)),
                    bv4=np.ascontiguousarray(bv[hs].reshape(1, 256)),
                    mb=MBv, iden=IDEN))
        res = run8(nc_pb1, maps)
        HOf = [np.zeros((D, S), bf16) for _ in range(B)]
        for b in range(B):
            for c in range(NCH):
                HOf[b][256 * c:256 * (c + 1), :] = res[_core(b, c)]["ho"]
        nc_pb2 = get_prog("pb2", build_pb2, CH)
        maps = [dict(x=np.ascontiguousarray(X[b][:, c * CH:(c + 1) * CH]),
                     hoc=np.ascontiguousarray(HOf[b][:, c * CH:(c + 1) * CH]),
                     wc=inp["L0_attn_Wc"].astype(np.float32),
                     bc=inp["L0_attn_bc"].reshape(D, 1).astype(np.float32))
                for b in range(B) for c in range(NCH)]
        res = run8(nc_pb2, maps)
        for b in range(B):
            for c in range(NCH):
                X[b][:, c * CH:(c + 1) * CH] = res[_core(b, c)]["out"]

    def l1_ffn_phase():
        HC = 1024
        nc_pd1 = get_prog("pd1", build_pd1, HC)
        maps = []
        for b in range(B):
            for c in range(NCH):
                for hh2 in range(2):
                    maps.append(dict(
                        x=np.ascontiguousarray(
                            X[b][:, c * CH + hh2 * HC:c * CH + (hh2 + 1) * HC]),
                        wq=inp["L1_ffn_W_Q"].astype(np.float32),
                        wk=inp["L1_ffn_W_K"].astype(np.float32),
                        wi=inp["L1_ffn_W_inv"].astype(np.float32), iden=IDEN))
        res1 = []
        for hh2 in range(2):
            sub = [maps[2 * i + hh2] for i in range(8)]
            res1.append(run8(nc_pd1, sub))
        nc_pd2 = get_prog("pd2", build_pd2, HC)
        res2 = []
        for hh2 in range(2):
            maps2 = []
            for b in range(B):
                for c in range(NCH):
                    sidx = 2 * c + hh2
                    prevs = np.zeros((16, D), np.float32)
                    pmk = np.zeros((16, 1), np.float32)
                    pmi = np.zeros((16, 1), np.float32)
                    for sp in range(8):
                        ks2 = res1[sp % 2][_core(b, sp // 2)]["ks"]
                        prevs[2 * sp] = ks2[0]
                        prevs[2 * sp + 1] = ks2[1]
                        if sp < sidx:
                            pmk[2 * sp, 0] = 1.0
                            pmi[2 * sp + 1, 0] = 1.0
                    r1 = res1[hh2][_core(b, c)]
                    maps2.append(dict(
                        x=np.ascontiguousarray(
                            X[b][:, c * CH + hh2 * HC:c * CH + (hh2 + 1) * HC]),
                        qh=r1["qo"], kh=r1["ko"], iqh=r1["iqo"], ikh=r1["iko"],
                        prevs=prevs, pmk=pmk, pmi=pmi,
                        vf=inp["L1_ffn_V_fwd"].astype(np.float32),
                        wf=inp["L1_ffn_W_fwd"].astype(np.float32),
                        uf=inp["L1_ffn_U_fwd"].astype(np.float32),
                        bfw=inp["L1_ffn_b_fwd"].astype(np.float32),
                        vi=inp["L1_ffn_V_inv"].astype(np.float32),
                        wie=inp["L1_ffn_W_inv_exp"].astype(np.float32),
                        ui=inp["L1_ffn_U_inv"].astype(np.float32),
                        bi=inp["L1_ffn_b_inv"].astype(np.float32),
                        rw1=inp["L1_ffn_router_w1"].astype(np.float32),
                        rb1=inp["L1_ffn_router_b1"].reshape(D, 1).astype(np.float32),
                        rw2=inp["L1_ffn_router_w2"].astype(np.float32),
                        rb2=inp["L1_ffn_router_b2"].reshape(8, 1).astype(np.float32),
                        eb=inp["L1_ffn_expert_bias"].reshape(8, 1).astype(np.float32),
                        alpha=np.full((128, 1), np.float32(inp["L1_ffn_alpha_bi"])),
                        nv=np.arange(sidx * HC + 1, sidx * HC + HC + 1,
                                     dtype=np.float32).reshape(1, HC),
                        rep=REP, iden=IDEN,
                        wo=inp["L1_ffn_W_O"].astype(np.float32)))
            res2.append(run8(nc_pd2, maps2))
        for b in range(B):
            for c in range(NCH):
                for hh2 in range(2):
                    X[b][:, c * CH + hh2 * HC:c * CH + (hh2 + 1) * HC] = \
                        res2[hh2][_core(b, c)]["out"]

    def l1_attn_phase():
        HC = 1024
        nc_pe1 = get_prog("pe1", build_pe1, HC)
        res1 = []
        for hh2 in range(2):
            maps = [dict(x=np.ascontiguousarray(
                        X[b][:, c * CH + hh2 * HC:c * CH + (hh2 + 1) * HC]),
                        wqkv=inp["L1_attn_w_qkv"].astype(np.float32),
                        proj=inp["L1_attn_proj_matrix"].astype(np.float32))
                    for b in range(B) for c in range(NCH)]
            res1.append(run8(nc_pe1, maps))
        nc_pe2 = get_prog("pe2", build_pe2, CH)
        maps2 = []
        for b in range(B):
            for c in range(NCH):
                kvp = np.stack([res1[sp % 2][_core(b, sp // 2)]["kvo"]
                                for sp in range(8)])
                qp = np.concatenate([res1[0][_core(b, c)]["qpo"],
                                     res1[1][_core(b, c)]["qpo"]], axis=1)
                maps2.append(dict(
                    x=np.ascontiguousarray(X[b][:, c * CH:(c + 1) * CH]),
                    qp=qp, kvp=kvp,
                    wout=inp["L1_attn_w_out"].astype(np.float32),
                    bout=inp["L1_attn_b_out"].reshape(D, 1).astype(np.float32)))
        res2 = run8(nc_pe2, maps2)
        for b in range(B):
            for c in range(NCH):
                X[b][:, c * CH:(c + 1) * CH] = res2[_core(b, c)]["out"]

    l0_ffn_phase()
    l0_attn_phase()
    l0_ffn_phase()
    l1_ffn_phase()
    l1_attn_phase()
    l1_ffn_phase()
    return np.ascontiguousarray(
        np.stack([X[0].T, X[1].T]).astype(np.float32))
